# version 42 (final)
# baseline (speedup 1.0000x reference)
import numpy as np
import os
WG_SKIP = os.environ.get("WG_SKIP", "1") == "1"
SERIAL = os.environ.get("SERIAL", "0") == "1"
WG_MODE = int(os.environ.get("WG_MODE", "3"))
WG_QUIESCE = os.environ.get("WG_Q", "1") == "1"
from contextlib import ExitStack

import concourse.bass as bass
from bass_rust import IndirectOffsetOnAxis as IOA
import concourse.mybir as mybir
from concourse.bass_utils import run_bass_kernel_spmd

F32 = mybir.dt.float32
U32 = mybir.dt.uint32
BF16 = mybir.dt.bfloat16
AF = mybir.ActivationFunctionType
ALU = mybir.AluOpType
AX = mybir.AxisListType

S = 2048
DM = 1024
NT = 16
H = 16
DH = 64
WIN = 32
NSLOT = 39
E = 8
HF = 2048
EPS_LN = 1e-5
EPS_NORM = 1e-7
NEG_BIG = -1.0e30
I1K = 1.0 / DM


class Op:
    __slots__ = ("i", "eng", "emit", "deps", "dma", "tag", "need_inc", "val",
                 "risky", "chain", "qwait")

    def __init__(self, i, eng, emit, deps, dma, tag, risky=False):
        self.i, self.eng, self.emit, self.deps = i, eng, emit, deps
        self.dma, self.tag = dma, tag
        self.need_inc = False
        self.val = None
        self.risky = risky
        self.chain = True
        self.qwait = None


class Sched:
    def __init__(self):
        self.ops = []

    def add(self, eng, emit, deps=(), dma=False, tag="", risky=False,
            chain=True):
        deps = [d for d in deps if d is not None]
        o = Op(len(self.ops), eng, emit, deps, dma, tag, risky)
        o.chain = chain
        for d in deps:
            assert d.i < o.i, f"forward dep {d.tag} -> {o.tag}"
        self.ops.append(o)
        return o

    def emit(self, block, sems):
        pending = {}
        for o in self.ops:
            if o.dma:
                continue
            p = pending.get(o.eng)
            if p is not None:
                o.deps.append(p)
                pending[o.eng] = None
            if o.risky:
                o.need_inc = True
                if o.chain:
                    pending[o.eng] = o
        for o in self.ops:
            if o.dma:
                o.need_inc = True
            for d in o.deps:
                if d.dma or d.eng != o.eng:
                    d.need_inc = True
        counts = {}
        for o in self.ops:
            if o.need_inc:
                key = ("dma", o.eng) if o.dma else o.eng
                counts[key] = counts.get(key, 0) + 1
                o.val = counts[key] * (16 if o.dma else 1)

        def semkey(o):
            return ("dma", o.eng) if o.dma else o.eng

        obs = {}

        def dep_val(d):
            return obs[d.i] if d.dma else d.val

        for ename in ("sync", "gpsimd"):
            waited = {}
            batch = []

            def close():
                if batch:
                    end = batch[-1].val
                    for b in batch:
                        obs[b.i] = end
                    batch.clear()
                    return end
                return None

            for o in self.ops:
                if o.eng != ename:
                    continue
                need_close = any(d.dma and d.eng == ename and d.i not in obs
                                 for d in o.deps)
                emits = False
                for d in o.deps:
                    if not d.dma and d.eng == ename and not d.risky:
                        continue
                    k = semkey(d)
                    if d.dma and d.eng == ename and d.i not in obs:
                        continue
                    if d.dma and d.eng != ename:
                        assert d.i in obs, (
                            f"dma dep {d.tag} of {o.tag} not yet batched "
                            f"(cross-queue open batch?)")
                    v = dep_val(d)
                    if waited.get(k, -1) >= v:
                        continue
                    emits = True
                    waited[k] = v
                if emits or need_close:
                    end = close()
                    if end is not None and ename == "sync":
                        o.qwait = end
                        waited[("dma", ename)] = max(
                            waited.get(("dma", ename), -1), end)
                if need_close:
                    for d in o.deps:
                        if d.dma and d.eng == ename:
                            k = semkey(d)
                            v = dep_val(d)
                            if waited.get(k, -1) < v:
                                waited[k] = v
                if o.dma:
                    batch.append(o)
            close()

        def run_engine(eng_obj, eng_name):
            waited = {}
            for o in self.ops:
                if o.eng != eng_name:
                    continue
                if o.qwait is not None:
                    qk = ("dma", eng_name)
                    if waited.get(qk, -1) < o.qwait:
                        eng_obj.wait_ge(sems[qk], o.qwait)
                        waited[qk] = o.qwait
                for d in o.deps:
                    if not d.dma and d.eng == eng_name and not d.risky:
                        continue
                    k = semkey(d)
                    v = dep_val(d)
                    assert v is not None, f"dep {d.tag} of {o.tag} has no val"
                    if waited.get(k, -1) >= v:
                        continue
                    eng_obj.wait_ge(sems[k], v)
                    waited[k] = v
                inst = o.emit(eng_obj)
                if o.need_inc:
                    assert inst is not None, f"op {o.tag} needs inc, emitted nothing"
                    inst.then_inc(sems[semkey(o)], 16 if o.dma else 1)

        @block.tensor
        def _(e):
            run_engine(e, "tensor")

        @block.vector
        def _(e):
            run_engine(e, "vector")

        @block.scalar
        def _(e):
            run_engine(e, "scalar")

        @block.sync
        def _(e):
            run_engine(e, "sync")

        @block.gpsimd
        def _(e):
            run_engine(e, "gpsimd")


def build_nc(phase="full", cut=99, dbg=False):
    nc = bass.Bass("TRN2", target_bir_lowering=False)

    x_d = nc.dram_tensor("x", [S, DM], F32, kind="ExternalInput")
    wq_d = nc.dram_tensor("wq", [DM, DM], F32, kind="ExternalInput")
    wk_d = nc.dram_tensor("wk", [DM, DM], F32, kind="ExternalInput")
    wkp_d = nc.dram_tensor("wkp", [DM, DM], F32, kind="ExternalInput")
    wv_d = nc.dram_tensor("wv", [DM, DM], F32, kind="ExternalInput")
    wo_d = nc.dram_tensor("wo", [DM, DM], F32, kind="ExternalInput")
    g2_d = nc.dram_tensor("g2", [DM], F32, kind="ExternalInput")
    be2_d = nc.dram_tensor("be2", [DM], F32, kind="ExternalInput")
    g3_d = nc.dram_tensor("g3", [DM], F32, kind="ExternalInput")
    b3_d = nc.dram_tensor("b3", [DM], F32, kind="ExternalInput")
    rw_d = nc.dram_tensor("rw", [DM, E], F32, kind="ExternalInput")
    rb_d = nc.dram_tensor("rb", [E], F32, kind="ExternalInput")
    w1_d = nc.dram_tensor("w1", [E, DM, HF], F32, kind="ExternalInput")
    b1_d = nc.dram_tensor("b1", [E, HF], F32, kind="ExternalInput")
    w2_d = nc.dram_tensor("w2", [E, HF, DM], F32, kind="ExternalInput")
    b2_d = nc.dram_tensor("b2", [E, DM], F32, kind="ExternalInput")
    mask_d = nc.dram_tensor("mask", [WIN, WIN * WIN], F32, kind="ExternalInput")
    ident_d = nc.dram_tensor("ident", [128, 128], F32, kind="ExternalInput")
    ltri_d = nc.dram_tensor("ltri", [128, 128], F32, kind="ExternalInput")
    tidt_d = nc.dram_tensor("tidt", [128, NT], F32, kind="ExternalInput")
    emaj_d = nc.dram_tensor("emaj", [128, 128], F32, kind="ExternalInput")
    def4_d = nc.dram_tensor("def4", [128, 4], F32, kind="ExternalInput")
    sio_d = nc.dram_tensor("sio", [1, 64], F32, kind="ExternalInput")
    wd1_d = nc.dram_tensor("wd1", [E * 128, 16 * 1024], BF16, kind="Internal")
    wd2_d = nc.dram_tensor("wd2", [E * 128, 16 * 1024], BF16, kind="Internal")
    zd_d = nc.dram_tensor("zd", [S, DM], F32, kind="Internal")
    slot_d = nc.dram_tensor("slotd", [NSLOT * 128, 4], F32, kind="Internal")
    abd_d = nc.dram_tensor("abd", [NSLOT * 128, DM], F32, kind="Internal")
    y_d = nc.dram_tensor("y", [S, DM], F32, kind="ExternalOutput")
    if dbg:
        dbg_wall = nc.dram_tensor("dbg_wall", [128, NT * E], F32, kind="ExternalOutput")
        dbg_stats = nc.dram_tensor("dbg_stats", [128, 4 * NT], F32, kind="ExternalOutput")
        dbg_abd = nc.dram_tensor("dbg_abd", [4352, DM], F32, kind="ExternalOutput")
        dbg_slotd = nc.dram_tensor("dbg_slotd", [NSLOT * 128, 4], F32, kind="ExternalOutput")
        dbg_zt = nc.dram_tensor("dbg_zt", [128, 8 * S], mybir.dt.bfloat16, kind="ExternalOutput")
        dbg_apad = nc.dram_tensor("dbg_apad", [128, DM], F32, kind="ExternalOutput")
        dbg_ztb = nc.dram_tensor("dbg_ztb", [128, 8 * WIN], mybir.dt.bfloat16, kind="ExternalOutput")
        dbg_mix = nc.dram_tensor("dbg_mix", [128, 8 * DM], F32, kind="ExternalOutput")

    with ExitStack() as ctx:
        def sb(name, shape, dtype=F32, c=None):
            return (c or ctx).enter_context(nc.sbuf_tensor(name, shape, dtype))

        attx = ExitStack()
        IDN = sb("idn", [128, 128])
        ONESR = sb("onesr", [1, 128])
        G2B = sb("g2b", [128, DM])
        B2LB = sb("b2lb", [128, DM])
        G3B = sb("g3b", [128, DM])
        B3B = sb("b3b", [128, DM])
        RW = sb("rw_sb", [128, 64])
        RB = sb("rb_sb", [E, 1])
        B1SA = sb("b1sa", [128, 128])
        B1T = sb("b1t", [128, 128])
        B2S = sb("b2s", [E, DM], BF16)
        MU1 = sb("mu1", [128, NT]); RSTD1 = sb("rstd1", [128, NT])
        MU2 = sb("mu2", [128, NT]); RSTD2 = sb("rstd2", [128, NT])
        WALL = sb("wall", [128, NT * E])
        XT = [sb("xt0", [128, DM]), sb("xt1", [128, DM])]
        XN = sb("xn", [128, DM])
        SQX = sb("sqx", [128, DM])
        TMP = sb("tmp", [128, DM])
        ZF = sb("zf", [128, DM])
        ZTF = sb("ztf", [128, DM])
        SCR = sb("scr", [128, 8])
        LGT = sb("lgt", [E, 128])
        RBR = sb("rbr", [1, E])
        S1T = sb("s1t", [128, NT]); S2T = sb("s2t", [128, NT])
        RT = sb("rt", [128, NT])
        NMR = sb("nmr", [128, 2 * NT])
        WTRALL = sb("wtrall", [E, NT * 128], BF16)
        LTE = sb("lte", [128, NT * E])
        EXB = sb("exb", [128, NT * E])
        AUX = sb("aux", [128, NT * E])
        TRE = sb("tre", [128, 128])
        APAD = sb("apad", [128, DM])
        MIX = sb("mix", [128, 4 * DM])
        OUTB = sb("outb", [128, 2 * DM])
        S0 = sb("s0", [128, 8 * 1024], BF16)
        S1 = sb("s1", [128, 8 * 1024], BF16)
        MASK = sb("maskt", [WIN, WIN * WIN], c=attx)
        QT = sb("qt", [128, 8 * WIN], BF16, c=attx)
        KT = sb("kt", [128, 8 * WIN], c=attx)
        KPT = sb("kpt", [128, 8 * WIN], c=attx)
        VT = sb("vt", [128, 8 * WIN], c=attx)
        MT = sb("mt", [128, WIN * WIN], BF16, c=attx)
        NTJ = sb("ntj", [128, 8 * DH], BF16, c=attx)
        PF = sb("pf", [WIN, WIN * WIN], c=attx)
        PSC = sb("psc", [WIN, WIN * WIN], c=attx)
        PT = sb("pt", [128, 8 * WIN], BF16, c=attx)
        ZTB = sb("ztb", [128, 8 * WIN], BF16, c=attx)
        PSR = sb("psr", [WIN, 1], c=attx)
        AOUT = sb("aout", [WIN, DM], c=attx)
        XWIN = sb("xwin", [WIN, DM], c=attx)

        PS_H = ctx.enter_context(nc.psum_tensor("ps_h", [128, 2048], F32))
        PS_A = ctx.enter_context(nc.psum_tensor("ps_a", [128, 1024], F32))
        PS_B = ctx.enter_context(nc.psum_tensor("ps_b", [128, 1024], F32))

        sems = {}
        for e in ("tensor", "vector", "scalar", "sync", "gpsimd"):
            sems[e] = ctx.enter_context(nc.semaphore(f"sem_{e}"))
        for e in ("sync", "gpsimd"):
            sems[("dma", e)] = ctx.enter_context(nc.semaphore(f"dsem_{e}"))
        block = ctx.enter_context(nc.Block())

        sc = Sched()
        V, A, T, SY, GP = "vector", "scalar", "tensor", "sync", "gpsimd"

        slots = {k: {"w": None, "r": []} for k in
                 ("H01", "H23", "A4", "A5", "B6", "B7")}

        def ps_w(names, op_deps):
            dd = list(op_deps)
            for n in names:
                dd += slots[n]["r"]
            return dd

        def ps_w_done(names, op):
            for n in names:
                slots[n]["w"] = op
                slots[n]["r"] = []

        def ps_r(names, op_deps):
            dd = list(op_deps)
            for n in names:
                if slots[n]["w"] is not None:
                    dd.append(slots[n]["w"])
            return dd

        def ps_r_done(names, op):
            for n in names:
                slots[n]["r"].append(op)

        ld_b2s = sc.add(GP, lambda e: e.dma_start(out=B2S[:], in_=b2_d.ap()),
                        dma=True, tag="ld_b2s")
        mz_ones = sc.add(V, lambda e: e.memset(ONESR[:], 1.0), tag="ones")
        mz_apad = sc.add(V, lambda e: e.memset(APAD[:], 0.0), tag="apad0")

        bc_stage = [XN[0:1, :], MIX[0:1, 0:DM], OUTB[0:1, 0:DM],
                    OUTB[0:1, DM:2 * DM]]
        bcast_cp = []
        for i, (d, dst) in enumerate(((g3_d, G3B), (b3_d, B3B), (g2_d, G2B), (be2_d, B2LB))):
            stg = bc_stage[i]
            ldp = sc.add(SY, lambda e, d=d, stg=stg: e.dma_start(
                out=stg, in_=bass.AP(d, 0, [[DM, 1], [1, DM]])),
                dma=True, tag=f"ld_prm{i}")

            def bcmm(e, stg=stg):
                for n in range(2):
                    last = e.matmul(PS_A[:, n * 512:(n + 1) * 512], ONESR[:],
                             stg[:, n * 512:(n + 1) * 512], start=True, stop=True)
                return last
            m = sc.add(T, bcmm, deps=ps_w(["A4", "A5"], [ldp, mz_ones]), tag=f"bc_mm{i}")
            ps_w_done(["A4", "A5"], m)
            cpo = sc.add(V, lambda e, dst=dst: e.tensor_copy(out=dst[:], in_=PS_A[:]),
                         deps=ps_r(["A4", "A5"], [m]), tag=f"bc_cp{i}")
            ps_r_done(["A4", "A5"], cpo)
            bcast_cp.append(cpo)

        def load_w_1024(slot, src_d, tag, deps):
            def emit(e):
                return e.dma_start(
                    out=slot[:].rearrange("p (kc j) -> p kc j", kc=8),
                    in_=src_d.ap().rearrange("(kc p) j -> p kc j", p=128))
            return sc.add(GP, emit, deps=deps, dma=True, tag=tag)


        def _finalize_partial():
            lasts = {}
            for o in sc.ops:
                lasts[o.eng] = o
            sc.add(SY, lambda e: None,
                   deps=[v for k, v in lasts.items() if k != SY] +
                        ([lasts[SY]] if lasts.get(SY) is not None and lasts[SY].dma else []),
                   tag="fin_partial")
            sc.emit(block, sems)

        if phase == "p0":
            _finalize_partial()
            return nc

        t_xn = [None] * NT
        zc_ops = [[None] * 8 for _ in range(NT)]
        t_w = [None] * NT
        last_xt_user = [None, None]
        last_ztr = None
        xwin_dma = None

        last_red = None
        p1_bufs = [XT[0], XT[1], TMP, ZF]
        p1_user = [None, None, None, None]
        for g in range(NT):
            bi = g % 4
            xt = p1_bufs[bi]
            xl = sc.add(SY if g % 2 == 0 else GP,
                lambda e, g=g, xt=xt: e.dma_start(
                    out=xt[:], in_=x_d.ap()[g * 128:(g + 1) * 128, :]),
                deps=[p1_user[bi]], dma=True, tag=f"xla{g}")
            r1 = sc.add(A, lambda e, xt=xt, g=g: e.activation(
                out=SQX[:], in_=xt[:], func=AF.Identity,
                accum_out=S1T[:, g:g + 1]),
                deps=[xl], tag=f"p1s{g}", risky=True)
            r2 = sc.add(A, lambda e, xt=xt, g=g: e.activation(
                out=SQX[:], in_=xt[:], func=AF.Square,
                accum_out=S2T[:, g:g + 1]),
                tag=f"p1ss{g}", risky=True)
            p1_user[bi] = r2
            last_red = r2
        last_xt_user[0] = p1_user[0]
        last_xt_user[1] = p1_user[1]

        ld_idn = sc.add(SY, lambda e: e.dma_start(out=IDN[:], in_=ident_d.ap()),
                        dma=True, tag="ld_idn")
        ld_rw = sc.add(SY, lambda e: e.dma_start(
            out=RW[:].rearrange("p (kc e) -> p kc e", e=E),
            in_=rw_d.ap().rearrange("(kc p) e -> p kc e", p=128)),
            dma=True, tag="ld_rw")
        ld_rb = sc.add(SY, lambda e: e.dma_start(
            out=RB[:], in_=bass.AP(rb_d, 0, [[1, E], [1, 1]])), dma=True, tag="ld_rb")
        ld_rbr = sc.add(SY, lambda e: e.dma_start(
            out=RBR[:], in_=bass.AP(rb_d, 0, [[E, 1], [1, E]])), dma=True, tag="ld_rbr")
        ldb1 = sc.add(SY, lambda e: e.dma_start(
            out=B1SA[:], in_=bass.AP(b1_d, 0, [[128, 128], [1, 128]])),
            dma=True, tag="ld_b1")
        trb1 = sc.add(T, lambda e: e.transpose(PS_B[:, 0:128], B1SA[:], IDN[:]),
                      deps=ps_w(["B6"], [ldb1, ld_idn]), tag="tr_b1")
        ps_w_done(["B6"], trb1)
        b1_last = sc.add(V, lambda e: e.tensor_copy(out=B1T[:], in_=PS_B[:, 0:128]),
                         deps=ps_r(["B6"], [trb1]), tag="cp_b1")
        ps_r_done(["B6"], b1_last)
        ld_wq = load_w_1024(S0, wq_d, "ld_wq", [p1_user[3]])
        ld_wk = load_w_1024(S1, wk_d, "ld_wk", [])

        pp_stores = []

        def emit_pp(lo, hi, deps0=()):
            i = 0
            for e_ in range(E):
                for ti in range(2):
                    for hf in range(2):
                        if not (lo <= i < hi):
                            i += 1
                            continue
                        if ti == 0:
                            inap = bass.AP(w1_d, e_ * DM * HF + hf * 4 * 128 * HF,
                                           [[HF, 128], [128 * HF, 4], [1, HF]])
                            oap = bass.AP(wd1_d, e_ * 128 * 16384 + hf * 8192,
                                          [[16384, 128], [2048, 4], [1, 2048]])
                        else:
                            inap = bass.AP(w2_d, e_ * HF * DM + hf * 8 * 128 * DM,
                                           [[DM, 128], [128 * DM, 8], [1, DM]])
                            oap = bass.AP(wd2_d, e_ * 128 * 16384 + hf * 8192,
                                          [[16384, 128], [1024, 8], [1, 1024]])
                        pp_stores.append(sc.add(
                            GP, lambda e, inap=inap, oap=oap: e.dma_start(
                                out=oap, in_=inap),
                            deps=(list(deps0) if i == lo else []),
                            dma=True, tag=f"pp{i}"))
                        i += 1


        emit_pp(0, 2, [p1_user[2]])

        sc.add(V, lambda e: e.tensor_scalar(
            out=MU1[:], in0=S1T[:], scalar1=I1K, scalar2=None, op0=ALU.mult),
            deps=[last_red], tag="b_mu1", risky=True)
        sc.add(V, lambda e: e.tensor_scalar(
            out=TRE[:, 0:16], in0=S2T[:], scalar1=I1K, scalar2=None, op0=ALU.mult),
            tag="b_ex2", risky=True)
        sc.add(V, lambda e: e.tensor_tensor(
            out=TRE[:, 16:32], in0=MU1[:], in1=MU1[:], op=ALU.mult),
            tag="b_musq", risky=True)
        b_var = sc.add(V, lambda e: e.tensor_tensor(
            out=TRE[:, 0:16], in0=TRE[:, 0:16], in1=TRE[:, 16:32], op=ALU.subtract),
            tag="b_var", risky=True)
        sc.add(A, lambda e: e.activation(
            out=TRE[:, 16:32], in_=TRE[:, 0:16], func=AF.Ln, bias=np.float32(EPS_LN),
            scale=1.0), deps=[b_var], tag="b_ln1")
        a_r1 = sc.add(A, lambda e: e.activation(
            out=RSTD1[:], in_=TRE[:, 16:32], func=AF.Exp, scale=-0.5), tag="b_r1")
        sc.add(V, lambda e: e.tensor_scalar(
            out=MU2[:], in0=S1T[:], scalar1=I1K, scalar2=None, op0=ALU.mult),
            deps=[a_r1], tag="b_mu2a", risky=True)
        sc.add(V, lambda e: e.tensor_tensor(
            out=MU2[:], in0=MU2[:], in1=MU1[:], op=ALU.subtract), tag="b_mu2b", risky=True)
        sc.add(V, lambda e: e.tensor_tensor(
            out=MU2[:], in0=MU2[:], in1=RSTD1[:], op=ALU.mult), tag="b_mu2c", risky=True)
        sc.add(V, lambda e: e.tensor_tensor(
            out=TRE[:, 16:32], in0=RSTD1[:], in1=RSTD1[:], op=ALU.mult),
            tag="b_rsq", risky=True)
        b_v2 = sc.add(V, lambda e: e.tensor_tensor(
            out=TRE[:, 16:32], in0=TRE[:, 16:32], in1=TRE[:, 0:16], op=ALU.mult),
            tag="b_v2", risky=True)
        sc.add(A, lambda e: e.activation(
            out=TRE[:, 32:48], in_=TRE[:, 16:32], func=AF.Ln, bias=np.float32(EPS_LN),
            scale=1.0), deps=[b_v2], tag="b_ln2")
        a_r2 = sc.add(A, lambda e: e.activation(
            out=RSTD2[:], in_=TRE[:, 32:48], func=AF.Exp, scale=-0.5), tag="b_r2")
        b_rt = sc.add(V, lambda e: e.tensor_tensor(
            out=RT[:], in0=RSTD1[:], in1=RSTD2[:], op=ALU.mult),
            deps=[a_r2], tag="b_rt", risky=True)
        b_nmr1 = sc.add(V, lambda e: e.scalar_tensor_tensor(
            out=NMR[:, 0:NT], in0=MU1[:], scalar=-1.0, in1=RSTD1[:],
            op0=ALU.mult, op1=ALU.mult), tag="b_nmr1", risky=True)
        b_nmr2 = sc.add(V, lambda e: e.scalar_tensor_tensor(
            out=NMR[:, NT:2 * NT], in0=MU2[:], scalar=-1.0, in1=RSTD2[:],
            op0=ALU.mult, op1=ALU.mult), tag="b_nmr2", risky=True)

        p2st = {"rmm_prev": None, "lgt_prev": None,
                "last_tmp_reader": [p1_user[2], None],
                "lte_ops": [], "last_ztr": [p1_user[3], last_red],
                "p1_zb_last": None,
                "xwin_dma": None, "last_zd": [None, None], "zd_ops": [],
                "rmm2": None}

        def emit_p2_tile(g):
            par = g % 2
            xl = sc.add(SY, lambda e, g=g, par=par: e.dma_start(
                out=XT[par][:], in_=x_d.ap()[g * 128:(g + 1) * 128, :]),
                deps=[last_xt_user[par]], dma=True, tag=f"xlb{g}")
            xt = XT[par]
            wpar = 0 if g == 15 else g % 2
            WB = TMP if wpar == 0 else XN
            z1 = sc.add(V, lambda e, xt=xt, g=g, WB=WB: e.tensor_scalar(
                out=WB[:], in0=xt[:], scalar1=MU1[:, g:g + 1],
                scalar2=RT[:, g:g + 1], op0=ALU.subtract, op1=ALU.mult),
                deps=[xl, b_rt, p2st["last_tmp_reader"][wpar]] +
                     ([p2st["xwin_dma"]] if wpar == 1 else []), tag=f"z1_{g}")
            last_xt_user[par] = z1
            if g == 15:
                t_xn[15] = sc.add(V, lambda e, xt=xt: e.tensor_scalar(
                    out=XN[:], in0=xt[:], scalar1=MU1[:, 15:16],
                    scalar2=RSTD1[:, 15:16], op0=ALU.subtract, op1=ALU.mult),
                    tag="xn15")
                last_xt_user[par] = t_xn[15]
                p2st["xwin_dma"] = sc.add(SY, lambda e: e.dma_start(
                    out=XWIN[:], in_=XN[96:128, :]), deps=[t_xn[15]], dma=True,
                    tag="xwin_dma")
            zpar = g % 2
            ZB = ZF if zpar == 0 else SQX
            tsl = ["B6", "B7"] if zpar == 0 else ["H01", "H23"]
            tps = PS_B if zpar == 0 else PS_H
            ZTFl = ZTF if zpar == 0 else APAD
            zg = sc.add(V, lambda e, ZB=ZB, WB=WB: e.tensor_tensor(
                out=ZB[:], in0=WB[:], in1=G3B[:], op=ALU.mult),
                deps=[z1, bcast_cp[0], p2st["last_ztr"][zpar],
                      p2st["last_zd"][zpar]],
                tag=f"zg_{g}")
            p2st["last_tmp_reader"][wpar] = zg
            zb = sc.add(V, lambda e, ZB=ZB: e.tensor_tensor(
                out=ZB[:], in0=ZB[:], in1=B3B[:], op=ALU.add),
                deps=[bcast_cp[1]], tag=f"zb_{g}")
            p2st["p1_zb_last"] = zb
            tr = None
            wdeps = ps_w(tsl, [zb, ld_idn])
            for c in range(8):
                tr = sc.add(T, lambda e, c=c, ZB=ZB, tps=tps: e.transpose(
                    tps[:, c * 128:(c + 1) * 128], ZB[:, c * 128:(c + 1) * 128],
                    IDN[:]), deps=wdeps if c == 0 else [], tag=f"ztr_{g}_{c}")
            ps_w_done(tsl, tr)
            p2st["last_ztr"][zpar] = tr
            zdw = sc.add(SY, lambda e, g=g, ZB=ZB: e.dma_start(
                out=zd_d.ap()[g * 128:(g + 1) * 128, :], in_=ZB[:]),
                deps=[zb], dma=True, tag=f"zd_{g}")
            p2st["last_zd"][zpar] = zdw
            p2st["zd_ops"].append(zdw)
            zcf = sc.add(A, lambda e, ZTFl=ZTFl, tps=tps: e.copy(
                ZTFl[:], tps[:, 0:1024]),
                deps=ps_r(tsl, [p2st["rmm2" if zpar else "rmm_prev"], tr]),
                tag=f"zcf_{g}")
            ps_r_done(tsl, zcf)

            def rmm(e, ZTFl=ZTFl):
                for kc in range(8):
                    e.matmul(PS_A[0:8, 0:128], RW[:, kc * 8:(kc + 1) * 8],
                             ZTFl[:, kc * 128:(kc + 1) * 128],
                             start=(kc == 0), stop=False)
                return e.matmul(PS_A[0:8, 0:128], RBR[:], ONESR[:],
                                start=False, stop=True)
            rmm_o = sc.add(T, rmm, deps=ps_w(["A4"], [zcf, ld_rw, ld_rbr, mz_ones]),
                           tag=f"rmm_{g}")
            ps_w_done(["A4"], rmm_o)
            p2st["rmm2" if zpar else "rmm_prev"] = rmm_o
            lgt_cp = sc.add(A, lambda e: e.copy(LGT[:], PS_A[0:8, 0:128]),
                            deps=ps_r(["A4"], [rmm_o, p2st["lgt_prev"]]), tag=f"lgt_{g}")
            ps_r_done(["A4"], lgt_cp)
            trl = sc.add(T, lambda e: e.transpose(
                PS_A[:, 512:520], LGT[:], IDN[0:8, 0:8]),
                deps=ps_w(["A5"], [lgt_cp]), tag=f"trl_{g}")
            ps_w_done(["A5"], trl)
            p2st["lgt_prev"] = trl
            lte = sc.add(A, lambda e, g=g: e.copy(
                bass.AP(LTE, g, [[NT * E, 128], [NT, E]]), PS_A[:, 512:520]),
                deps=ps_r(["A5"], [trl]), tag=f"lte_{g}")
            ps_r_done(["A5"], lte)
            p2st["lte_ops"].append(lte)

        emit_p2_tile(15)
        xwin_dma = p2st["xwin_dma"]
        ld_mask = sc.add(SY, lambda e: e.dma_start(out=MASK[:], in_=mask_d.ap()),
                         dma=True, tag="ld_mask")
        with ExitStack() as actx:
            XWT = sb("xwt", [128, 8 * WIN], BF16, c=actx)
            TF = sb("tf", [WIN, DM], c=actx)
            SQA = sb("sqa", [WIN, DM], c=actx)
            NORMS = sb("norms", [WIN, 3 * H], c=actx)
            INVN = sb("invn", [WIN, 3 * H], c=actx)

            xwt_tr = None
            wdeps = ps_w(["A4"], [xwin_dma, ld_idn])
            for c in range(8):
                xwt_tr = sc.add(T, lambda e, c=c: e.transpose(
                    PS_A[:, c * 32:(c + 1) * 32],
                    XWIN[:, c * 128:(c + 1) * 128], IDN[0:32, 0:32]),
                    deps=wdeps if c == 0 else [], tag=f"xwtr_{c}")
            ps_w_done(["A4"], xwt_tr)
            xwt_cp = sc.add(A, lambda e: e.copy(XWT[:], PS_A[:, 0:256]),
                            deps=ps_r(["A4"], [xwt_tr]), tag="xwc")
            ps_r_done(["A4"], xwt_cp)
            a_xwt = [xwt_cp] * 8

            def proj_tok(wslot, wdep, psl_names, psl_ap, tag):
                def emit(e):
                    for n in range(2):
                        for kc in range(8):
                            last = e.matmul(psl_ap[0:32, n * 512:(n + 1) * 512],
                                     XWT[:, kc * 32:(kc + 1) * 32],
                                     wslot[:, kc * 1024 + n * 512: kc * 1024 + (n + 1) * 512],
                                     start=(kc == 0), stop=(kc == 7))
                    return last
                o = sc.add(T, emit, deps=ps_w(psl_names, [wdep, a_xwt[7]]), tag=tag)
                ps_w_done(psl_names, o)
                return o

            tr_par = [0]

            def small_tr(in_ap, out_cols, ident_ap, deps, tag):
                sl = "A4" if tr_par[0] == 0 else "A5"
                pv = PS_A[:, 0:out_cols] if tr_par[0] == 0 else PS_A[:, 512:512 + out_cols]
                tr_par[0] ^= 1
                o = sc.add(T, lambda e, pv=pv, in_ap=in_ap, ident_ap=ident_ap:
                           e.transpose(pv, in_ap, ident_ap),
                           deps=ps_w([sl], deps), tag=tag)
                ps_w_done([sl], o)
                return pv, o, sl

            def l2norm_and_T(dst, ev_dep, tag, stage, nc0):
                rs_last = None
                for h in range(H):
                    rs_last = sc.add(A, lambda e, h=h, stage=stage, nc0=nc0:
                        e.activation(
                            out=XWIN[:, h * DH:(h + 1) * DH],
                            in_=stage[:, h * DH:(h + 1) * DH], func=AF.Square,
                            accum_out=NORMS[:, nc0 + h:nc0 + h + 1]),
                        deps=[ev_dep] if h == 0 else [],
                        tag=f"{tag}_rs{h}", risky=(h == H - 1))
                sc.add(A, lambda e, nc0=nc0: e.activation(
                    out=INVN[:, nc0:nc0 + H], in_=NORMS[:, nc0:nc0 + H],
                    func=AF.Ln), tag=f"{tag}_ln")
                aex = sc.add(A, lambda e, nc0=nc0: e.activation(
                    out=NORMS[:, nc0:nc0 + H], in_=INVN[:, nc0:nc0 + H],
                    func=AF.Exp, scale=0.5), tag=f"{tag}_ex")
                sc.add(V, lambda e, nc0=nc0: e.tensor_scalar(
                    out=INVN[:, nc0:nc0 + H], in0=NORMS[:, nc0:nc0 + H],
                    scalar1=EPS_NORM, scalar2=None,
                    op0=ALU.add), deps=[aex], tag=f"{tag}_eps", risky=True)
                sc.add(V, lambda e, nc0=nc0: e.reciprocal(
                    out=NORMS[:, nc0:nc0 + H], in_=INVN[:, nc0:nc0 + H]),
                    tag=f"{tag}_rcp", risky=True)
                scale_last = sc.add(V, lambda e, stage=stage, nc0=nc0: e.tensor_tensor(
                    out=stage[:].rearrange("p (h d) -> p h d", d=DH),
                    in0=stage[:].rearrange("p (h d) -> p h d", d=DH),
                    in1=bass.AP(NORMS, nc0, [[3 * H, WIN], [1, H], [0, DH]]),
                    op=ALU.mult), tag=f"{tag}_scale")
                last_tr = None
                wdeps = ps_w(["A4"], [scale_last])
                for c in range(8):
                    last_tr = sc.add(T, lambda e, c=c, stage=stage: e.transpose(
                        PS_A[:, c * 32:(c + 1) * 32],
                        stage[:, c * 128:(c + 1) * 128], IDN[0:32, 0:32]),
                        deps=wdeps if c == 0 else [], tag=f"{tag}_tr{c}")
                ps_w_done(["A4"], last_tr)
                ev = sc.add(A, lambda e: e.copy(dst[:], PS_A[:, 0:256]),
                            deps=ps_r(["A4"], [last_tr]), tag=f"{tag}_ev")
                ps_r_done(["A4"], ev)
                evs = [ev] * 8
                return last_tr, evs

            qmm = proj_tok(S0, ld_wq, ["H01"], PS_H[:, 0:1024], "qmm")
            q_ev = sc.add(A, lambda e: e.copy(
                TF[:], PS_H[0:32, 0:1024]),
                deps=ps_r(["H01"], [qmm]), tag="q_ev")
            ps_r_done(["H01"], q_ev)
            q_trl, q_evs = l2norm_and_T(QT, q_ev, "qn", TF, 0)

            kmm = proj_tok(S1, ld_wk, ["H23"], PS_H[:, 1024:2048], "kmm")
            k_ev = sc.add(A, lambda e: e.copy(
                SQA[:], PS_H[0:32, 1024:2048]),
                deps=ps_r(["H23"], [kmm]), tag="k_ev")
            ps_r_done(["H23"], k_ev)
            ld_wkp = load_w_1024(S0, wkp_d, "ld_wkp", [qmm])
            k_trl, k_evs = l2norm_and_T(KT, k_ev, "kn", SQA, H)

            ld_wv = load_w_1024(S1, wv_d, "ld_wv", [kmm])
            emit_pp(2, 12, [k_ev])
            kpmm = proj_tok(S0, ld_wkp, ["B6", "B7"], PS_B[:, 0:1024], "kpmm")
            kp_ev = sc.add(A, lambda e: e.copy(
                TF[:], PS_B[0:32, 0:1024]),
                deps=ps_r(["B6", "B7"], [kpmm, q_trl]), tag="kp_ev")
            ps_r_done(["B6", "B7"], kp_ev)
            kp_trl, kp_evs = l2norm_and_T(KPT, kp_ev, "kpn", TF, 2 * H)

            def vmm_emit(e):
                for fc in range(8):
                    for kc in range(8):
                        last = e.matmul(PS_B[:, fc * 32:(fc + 1) * 32],
                                 S1[:, kc * 1024 + fc * 128: kc * 1024 + fc * 128 + 128],
                                 XWT[:, kc * 32:(kc + 1) * 32],
                                 start=(kc == 0), stop=(kc == 7))
                return last
            vmm = sc.add(T, vmm_emit, deps=ps_w(["B6"], [ld_wv, a_xwt[7]]), tag="vmm")
            ps_w_done(["B6"], vmm)
            v_ev = sc.add(A, lambda e: e.copy(VT[:], PS_B[:, 0:256]),
                          deps=ps_r(["B6"], [vmm]), tag="v_ev")
            ps_r_done(["B6"], v_ev)

        ld_wo = load_w_1024(S1, wo_d, "ld_wo", [vmm])
        emit_pp(12, 32, [v_ev])

        for g in range(15):
            emit_p2_tile(g)
        mz_apad2 = sc.add(V, lambda e: e.memset(APAD[:], 0.0),
                          deps=[p2st["rmm2"]], tag="apad0b")

        smst = {}

        def emit_softmax():
            def nmax_exp(off):
                return bass.AP(TRE, off, [[128, 128], [0, E], [1, NT]])
            t1 = sc.add(V, lambda e: e.tensor_tensor(
                out=TRE[:, 0:64], in0=LTE[:, 0:64], in1=LTE[:, 64:128], op=ALU.max),
                deps=p2st["lte_ops"], tag="sm_t1", risky=True)
            sc.add(V, lambda e: e.tensor_tensor(
                out=TRE[:, 64:96], in0=TRE[:, 0:32], in1=TRE[:, 32:64], op=ALU.max),
                tag="sm_t2", risky=True)
            sc.add(V, lambda e: e.tensor_tensor(
                out=TRE[:, 96:112], in0=TRE[:, 64:80], in1=TRE[:, 80:96], op=ALU.max),
                tag="sm_t3", risky=True)
            sm_sub = sc.add(V, lambda e: e.tensor_tensor(
                out=EXB[:], in0=LTE[:], in1=nmax_exp(96), op=ALU.subtract),
                tag="sm_sub", risky=True)
            sm_exp = sc.add(A, lambda e: e.activation(
                out=AUX[:], in_=EXB[:], func=AF.Exp), deps=[sm_sub], tag="sm_exp")
            sc.add(V, lambda e: e.tensor_scalar(
                out=LTE[:], in0=EXB[:], scalar1=0.0, scalar2=None, op0=ALU.is_ge),
                deps=[sm_exp], tag="sm_msk", risky=True)
            sc.add(V, lambda e: e.tensor_scalar(
                out=LTE[:], in0=LTE[:], scalar1=-1.0, scalar2=1.0,
                op0=ALU.mult, op1=ALU.add), tag="sm_inv", risky=True)
            sc.add(V, lambda e: e.tensor_tensor(
                out=EXB[:], in0=AUX[:], in1=LTE[:], op=ALU.mult), tag="sm_p2",
                risky=True)
            sc.add(V, lambda e: e.tensor_tensor(
                out=TRE[:, 0:64], in0=EXB[:, 0:64], in1=EXB[:, 64:128], op=ALU.max),
                tag="sm_m1", risky=True)
            sc.add(V, lambda e: e.tensor_tensor(
                out=TRE[:, 64:96], in0=TRE[:, 0:32], in1=TRE[:, 32:64], op=ALU.max),
                tag="sm_m2", risky=True)
            sc.add(V, lambda e: e.tensor_tensor(
                out=TRE[:, 112:128], in0=TRE[:, 64:80], in1=TRE[:, 80:96], op=ALU.max),
                tag="sm_m3", risky=True)
            sc.add(V, lambda e: e.tensor_tensor(
                out=LTE[:], in0=AUX[:], in1=nmax_exp(112), op=ALU.is_ge),
                tag="sm_sel", risky=True)
            sc.add(V, lambda e: e.tensor_tensor(
                out=EXB[:], in0=AUX[:], in1=LTE[:], op=ALU.mult), tag="sm_pw",
                risky=True)
            sc.add(V, lambda e: e.tensor_tensor(
                out=TRE[:, 0:64], in0=EXB[:, 0:64], in1=EXB[:, 64:128], op=ALU.add),
                tag="sm_s1", risky=True)
            sc.add(V, lambda e: e.tensor_tensor(
                out=TRE[:, 64:96], in0=TRE[:, 0:32], in1=TRE[:, 32:64], op=ALU.add),
                tag="sm_s2", risky=True)
            sc.add(V, lambda e: e.tensor_tensor(
                out=TRE[:, 96:112], in0=TRE[:, 64:80], in1=TRE[:, 80:96], op=ALU.add),
                tag="sm_s3", risky=True)
            sc.add(V, lambda e: e.tensor_scalar(
                out=TRE[:, 96:112], in0=TRE[:, 96:112], scalar1=1e-8, scalar2=None,
                op0=ALU.max), tag="sm_cl", risky=True)
            sc.add(V, lambda e: e.reciprocal(
                out=TRE[:, 112:128], in_=TRE[:, 96:112]), tag="sm_rcp", risky=True)
            smst["w_fin"] = sc.add(V, lambda e: e.tensor_tensor(
                out=bass.AP(WALL, 0, [[NT * E, 128], [1, E], [E, NT]]),
                in0=EXB[:], in1=nmax_exp(112), op=ALU.mult), tag="sm_w", risky=True)
            for g in range(NT):
                t_w[g] = smst["w_fin"]


        if phase in ("p1", "tile1"):
            _finalize_partial()
            return nc


        zev = [None] * H
        atst = {"mt_readers": [], "ntj_readers": [], "prev_zmm": None}

        def emit_attn_head_a(c, hp):
            mt_readers = atst["mt_readers"]
            ntj_readers = atst["ntj_readers"]
            if hp == 0:
                mb_last = sc.add(V, lambda e, c=c: e.tensor_tensor(
                    out=MT[:].rearrange("p (j k) -> p j k", k=WIN),
                    in0=bass.AP(KT, WIN * c, [[8 * WIN, 128], [1, WIN], [0, WIN]]),
                    in1=bass.AP(KPT, WIN * c, [[8 * WIN, 128], [0, WIN], [1, WIN]]),
                    op=ALU.mult),
                    deps=[k_evs[c], kp_evs[c]] + mt_readers, tag=f"mt_{c}")
                atst["mt_readers"] = []
                nb_last = sc.add(V, lambda e, c=c: e.tensor_tensor(
                    out=ZF[:, 0:WIN * WIN].rearrange("p (j k) -> p j k", k=WIN),
                    in0=bass.AP(VT, WIN * c, [[8 * WIN, 128], [1, WIN], [0, WIN]]),
                    in1=bass.AP(VT, WIN * c, [[8 * WIN, 128], [0, WIN], [1, WIN]]),
                    op=ALU.mult),
                    deps=[v_ev, p2st["p1_zb_last"]] + ntj_readers, tag=f"nt_{c}")
                atst["ntj_readers"] = []
                atst["mb_last"] = mb_last
                atst["nb_last"] = nb_last
            h = 2 * c + hp

            def amm_emit(e, c=c, hp=hp):
                b0 = 64 * hp
                for n in range(2):
                    last = e.matmul(PS_B[0:32, n * 512:(n + 1) * 512],
                             QT[b0:b0 + 64, c * 32:(c + 1) * 32],
                             MT[b0:b0 + 64, n * 512:(n + 1) * 512],
                             start=True, stop=True)
                return last
            amm = sc.add(T, amm_emit,
                         deps=ps_w(["B6", "B7"], [q_evs[c], atst["mb_last"]]),
                         tag=f"amm_{h}")
            ps_w_done(["B6", "B7"], amm)
            atst["mt_readers"].append(amm)
            t_msk = sc.add(V, lambda e: e.scalar_tensor_tensor(
                out=PF[:], in0=PS_B[0:32, 0:1024], scalar=0.125, in1=MASK[:],
                op0=ALU.mult, op1=ALU.add),
                deps=ps_r(["B6", "B7"], [amm, ld_mask]), tag=f"msk_{h}")
            ps_r_done(["B6", "B7"], t_msk)
            amax = sc.add(V, lambda e: e.reduce_max(
                out=SCR[0:32, 0:1], in_=PF[:], axis=AX.X, negate=True),
                tag=f"amax_{h}", risky=True)
            a_pexh = sc.add(A, lambda e: e.activation(
                out=PSC[:], in_=PF[:], func=AF.Exp, bias=SCR[0:32, 0:1],
                scale=1.0, accum_out=PSR[:]),
                deps=[amax], tag=f"aexp_{h}")
            sc.add(V, lambda e: e.reciprocal(
                out=SCR[0:32, 1:2], in_=PSR[:]), deps=[a_pexh],
                tag=f"arcp_{h}", risky=True)
            atst["t_psc"] = sc.add(V, lambda e: e.tensor_scalar(
                out=PF[:], in0=PSC[:], scalar1=SCR[0:32, 1:2],
                scalar2=None, op0=ALU.mult), tag=f"apn_{h}")

        def emit_attn_head_b(c, hp):
            h = 2 * c + hp
            t_psc = atst["t_psc"]
            nb_last = atst["nb_last"]
            prev_zmm = atst["prev_zmm"]
            ptr_last = None
            wdeps = ps_w(["B6"], [t_psc])
            for b in range(8):
                ptr_last = sc.add(T, lambda e, b=b: e.transpose(
                    PS_B[:, b * 32:(b + 1) * 32],
                    PF[:, b * 128:(b + 1) * 128], IDN[0:32, 0:32]),
                    deps=wdeps if b == 0 else [], tag=f"ptr_{h}_{b}")
            ps_w_done(["B6"], ptr_last)
            pt_last_ev = sc.add(A, lambda e: e.copy(PT[:], PS_B[:, 0:256]),
                                deps=ps_r(["B6"], [ptr_last, prev_zmm]),
                                tag=f"ptc_{h}")
            ps_r_done(["B6"], pt_last_ev)
            ntr_last = None
            wdeps = ps_w(["B7"], [nb_last])
            for b in range(8):
                ntr_last = sc.add(T, lambda e, b=b, hp=hp: e.transpose(
                    PS_B[:, 512 + b * 64:512 + (b + 1) * 64],
                    ZF[64 * hp:64 * hp + 64, b * 128:(b + 1) * 128],
                    IDN[64 * hp:64 * hp + 64, 64 * hp:64 * hp + 64]),
                    deps=wdeps if b == 0 else [], tag=f"ntr_{h}_{b}")
            ps_w_done(["B7"], ntr_last)
            ntj_last_ev = sc.add(A, lambda e: e.copy(NTJ[:], PS_B[:, 512:1024]),
                                 deps=ps_r(["B7"], [ntr_last]),
                                 tag=f"ntc_{h}")
            ps_r_done(["B7"], ntj_last_ev)

            def zmm_emit(e):
                for b in range(8):
                    last = e.matmul(PS_B[0:64, 0:32], NTJ[:, b * 64:(b + 1) * 64],
                             PT[:, b * 32:(b + 1) * 32],
                             start=(b == 0), stop=(b == 7))
                return last
            zmm = sc.add(T, zmm_emit,
                         deps=ps_w(["B6"], [pt_last_ev, ntj_last_ev]),
                         tag=f"zmm_{h}")
            ps_w_done(["B6"], zmm)
            atst["ntj_readers"].append(zmm)
            atst["prev_zmm"] = zmm
            zev[h] = sc.add(A, lambda e, hp=hp, c=c: e.copy(
                ZTB[64 * hp:64 * hp + 64, c * 32:(c + 1) * 32], PS_B[0:64, 0:32]),
                deps=ps_r(["B6"], [zmm]), tag=f"zev_{h}")
            ps_r_done(["B6"], zev[h])

        aost = {}

        def emit_attn_out_mm():
            def aomm_emit(e):
                for n in range(2):
                    for kc in range(8):
                        last = e.matmul(PS_B[0:32, n * 512:(n + 1) * 512],
                                 ZTB[:, kc * 32:(kc + 1) * 32],
                                 S1[:, kc * 1024 + n * 512: kc * 1024 + (n + 1) * 512],
                                 start=(kc == 0), stop=(kc == 7))
                return last
            aomm = sc.add(T, aomm_emit,
                          deps=ps_w(["B6", "B7"], [ld_wo] + zev), tag="aomm")
            ps_w_done(["B6", "B7"], aomm)
            a_aev = sc.add(A, lambda e: e.copy(AOUT[:], PS_B[0:32, 0:1024]),
                           deps=ps_r(["B6", "B7"], [aomm]), tag="a_aev")
            ps_r_done(["B6", "B7"], a_aev)
            aost["a_aev"] = a_aev
            apad_dma = sc.add(SY, lambda e: e.dma_start(
                out=APAD[96:128, :], in_=AOUT[:]),
                deps=[a_aev, mz_apad2], dma=True, tag="apad_dma")
            aost["apad_dma"] = apad_dma

        def emit_attn_out_f15():
            apad_dma = aost["apad_dma"]
            x15 = sc.add(SY, lambda e: e.dma_start(
                out=XT[1][:], in_=x_d.ap()[15 * 128:16 * 128, :]),
                deps=[last_xt_user[1]], dma=True, tag="x15")
            f_xn = sc.add(V, lambda e: e.tensor_scalar(
                out=XN[:], in0=XT[1][:], scalar1=MU1[:, 15:16],
                scalar2=RSTD1[:, 15:16], op0=ALU.subtract, op1=ALU.mult),
                deps=[x15, xwin_dma], tag="f15_xn")
            last_xt_user[1] = f_xn
            f_a = sc.add(V, lambda e: e.tensor_tensor(
                out=TMP[:], in0=XN[:], in1=APAD[:], op=ALU.add),
                deps=[apad_dma], tag="f15_a")
            sc.add(V, lambda e: e.tensor_tensor(
                out=SQX[:], in0=TMP[:], in1=TMP[:], op=ALU.mult),
                deps=[p2st["last_ztr"][1], p2st["last_zd"][1]], tag="f15_sq")
            sc.add(V, lambda e: e.reduce_sum(
                out=SCR[:, 0:1], in_=TMP[:], axis=AX.X), tag="f15_s", risky=True)
            sc.add(V, lambda e: e.tensor_scalar(
                out=MU2[:, 15:16], in0=SCR[:, 0:1], scalar1=I1K, scalar2=None,
                op0=ALU.mult), tag="f15_mu", risky=True)
            sc.add(V, lambda e: e.tensor_tensor(
                out=SCR[:, 1:2], in0=MU2[:, 15:16], in1=MU2[:, 15:16],
                op=ALU.mult), tag="f15_musq")
            sc.add(V, lambda e: e.reduce_sum(
                out=SCR[:, 2:3], in_=SQX[:], axis=AX.X), tag="f15_ss", risky=True)
            sc.add(V, lambda e: e.tensor_scalar(
                out=SCR[:, 2:3], in0=SCR[:, 2:3], scalar1=I1K, scalar2=None,
                op0=ALU.mult), tag="f15_va", risky=True)
            f_v = sc.add(V, lambda e: e.tensor_tensor(
                out=SCR[:, 2:3], in0=SCR[:, 2:3], in1=SCR[:, 1:2],
                op=ALU.subtract), tag="f15_vb")
            sc.add(A, lambda e: e.activation(
                out=SCR[:, 3:4], in_=SCR[:, 2:3], func=AF.Ln, bias=np.float32(EPS_LN),
                scale=1.0), deps=[f_v], tag="f15_ln")
            f_rstd = sc.add(A, lambda e: e.activation(
                out=RSTD2[:, 15:16], in_=SCR[:, 3:4], func=AF.Exp, scale=-0.5),
                tag="f15_rstd")
            nmr2f = sc.add(V, lambda e: e.scalar_tensor_tensor(
                out=NMR[:, NT + 15:NT + 16], in0=MU2[:, 15:16],
                scalar=-1.0, in1=RSTD2[:, 15:16],
                op0=ALU.mult, op1=ALU.mult), deps=[f_rstd], tag="f15_nmr2",
                risky=True)
            aost["apad_dma"] = apad_dma
            aost["f_rstd"] = f_rstd
            aost["nmr2f"] = nmr2f
            if dbg:
                sc.add(SY, lambda e: e.dma_start(out=dbg_apad.ap(), in_=APAD[:]),
                       deps=[apad_dma, f_a], dma=True, tag="dbg_apad")
                sc.add(SY, lambda e: e.dma_start(out=dbg_ztb.ap(), in_=ZTB[:]),
                       deps=zev, dma=True, tag="dbg_ztb")


        attx.close()
        W1S = sb("w1s", [128, 16 * 1024], BF16)
        W2S = sb("w2s", [128, 16 * 1024], BF16)
        IDNB = sb("idnb", [128, 128], BF16)
        LTRIS = sb("ltris", [128, 128])
        TIDTS = sb("tidts", [128, NT])
        EMAJS = sb("emajs", [128, 128])
        DEFS = sb("defs", [128, 4])
        SIOS = sb("sios", [1, 64])
        ONESC = sb("onesc", [128, 1])
        MSKE = sb("mske", [128, 128])
        RNKE = sb("rnke", [128, 128])
        M0E = sb("m0e", [128, 128])
        M1E = sb("m1e", [128, 128])
        PFXE = sb("pfxe", [128, 128])
        OFFBE = sb("offbe", [128, 128])
        POSE = sb("pose", [128, 128])
        RS1 = sb("rs1", [128, 128])
        FLD = sb("fld", [128, 96])
        ACCS = sb("accs", [128, 128])
        RW0 = sb("rowscr", [1, 1536])
        VREC = sb("vrec", [128, 2 * 64])
        IRECU = sb("irecu", [128, 2 * NT], U32)
        MASKE2 = sb("maske2", [128, 8 * NSLOT])
        EBF = sb("ebf", [128, 128])
        WIDXF = sb("widxf", [128, 128])
        WIDXU = sb("widxu", [128, 128], U32)
        SMET = sb("smet", [128, 16])
        SIDU = sb("sidu", [128, 8], U32)
        B1SL = sb("b1sl", [128, 32])

        st_ops = [None] * NT
        wtr_cp = [None] * NT

        def emit_wtr_chunk(gs):
            for g in gs:
                sl = "B6" if g % 2 == 0 else "B7"
                off = 0 if g % 2 == 0 else 512
                tr = sc.add(T, lambda e, g=g, off=off: e.transpose(
                    PS_B[0:8, off:off + 128], WALL[:, g * 8:(g + 1) * 8], IDN[:]),
                    deps=ps_w([sl], [t_w[g], ld_idn]), tag=f"wtr_{g}")
                ps_w_done([sl], tr)
                cp = sc.add(A, lambda e, g=g, off=off: e.copy(
                    WTRALL[0:8, g * 128:(g + 1) * 128], PS_B[0:8, off:off + 128]),
                    deps=ps_r([sl], [tr]), tag=f"wtrcp_{g}")
                ps_r_done([sl], cp)
                wtr_cp[g] = cp

        emit_softmax()
        for c_ in range(8):
            for hp_ in range(2):
                emit_attn_head_a(c_, hp_)
                emit_attn_head_b(c_, hp_)
        emit_attn_out_mm()
        emit_attn_out_f15()
        emit_wtr_chunk(range(NT))
        if dbg:
            sc.add(SY, lambda e: e.dma_start(out=dbg_wall.ap(), in_=WALL[:]),
                   deps=[smst["w_fin"]], dma=True, tag="dbg_wall")
            sc.add(SY, lambda e: e.dma_start(
                out=dbg_stats.ap()[:, 0:NT], in_=MU1[:]), deps=[smst["w_fin"]],
                dma=True, tag="dbg_mu1")
            sc.add(SY, lambda e: e.dma_start(
                out=dbg_stats.ap()[:, NT:2 * NT], in_=RSTD1[:]), dma=True, tag="dbg_r1")
            sc.add(SY, lambda e: e.dma_start(
                out=dbg_stats.ap()[:, 2 * NT:3 * NT], in_=MU2[:]), dma=True, tag="dbg_mu2")
            sc.add(SY, lambda e: e.dma_start(
                out=dbg_stats.ap()[:, 3 * NT:4 * NT], in_=RSTD2[:]), dma=True, tag="dbg_r2")

        ld_ltri = sc.add(SY, lambda e: e.dma_start(out=LTRIS[:], in_=ltri_d.ap()),
                         dma=True, tag="ld_ltri")
        ld_tidt = sc.add(SY, lambda e: e.dma_start(out=TIDTS[:], in_=tidt_d.ap()),
                         dma=True, tag="ld_tidt")
        ld_emaj = sc.add(SY, lambda e: e.dma_start(out=EMAJS[:], in_=emaj_d.ap()),
                         dma=True, tag="ld_emaj")
        ld_defs = sc.add(SY, lambda e: e.dma_start(out=DEFS[:], in_=def4_d.ap()),
                         dma=True, tag="ld_defs")
        ld_sio = sc.add(SY, lambda e: e.dma_start(out=SIOS[:], in_=sio_d.ap()),
                        dma=True, tag="ld_sio")
        mz_onesc = sc.add(V, lambda e: e.memset(ONESC[:], 1.0), tag="onesc")
        idnb_cp = sc.add(V, lambda e: e.tensor_copy(out=IDNB[:], in_=IDN[:]),
                         deps=[ld_idn], tag="idnb")
        def_dmas = []
        for s_ in range(NSLOT):
            def_dmas.append(sc.add(SY, lambda e, s_=s_: e.dma_start(
                out=slot_d.ap()[s_ * 128:(s_ + 1) * 128, :], in_=DEFS[:]),
                deps=[ld_defs], dma=True, tag=f"sdef{s_}"))

        breg_holder = {}

        def get_breg(e):
            if "r" not in breg_holder:
                rg = e.register("bnd")
                r = rg.__enter__()
                e.reg_mov(r, 1023)
                breg_holder["r"] = r
            return breg_holder["r"]

        def gp_bridge(deps, tag):
            return sc.add(GP, lambda e: e.tensor_copy(
                out=SCR[0:1, 7:8], in_=ONESR[0:1, 0:1]), deps=deps, tag=tag)

        wfin = smst["w_fin"]
        walle = bass.AP(WALL, 0, [[NT * E, 128], [1, E], [E, NT]])
        r_msk = sc.add(V, lambda e: e.tensor_scalar(
            out=MSKE[:].rearrange("p (e g) -> p e g", g=NT), in0=walle,
            scalar1=0.0, scalar2=None, op0=ALU.is_gt),
            deps=[wfin], tag="r_msk", risky=True)
        sc.add(V, lambda e: e.tensor_copy(out=RS1[:, 0:16], in_=MSKE[:, 0:16]),
               tag="r_p1a", risky=True)
        sc.add(V, lambda e: e.tensor_tensor(
            out=RS1[:, 16:128], in0=MSKE[:, 16:128], in1=MSKE[:, 0:112],
            op=ALU.add), tag="r_p1b", risky=True)
        sc.add(V, lambda e: e.tensor_copy(out=RNKE[:, 0:32], in_=RS1[:, 0:32]),
               tag="r_p2a", risky=True)
        sc.add(V, lambda e: e.tensor_tensor(
            out=RNKE[:, 32:128], in0=RS1[:, 32:128], in1=RS1[:, 0:96],
            op=ALU.add), tag="r_p2b", risky=True)
        sc.add(V, lambda e: e.tensor_copy(out=RS1[:, 0:64], in_=RNKE[:, 0:64]),
               tag="r_p3a", risky=True)
        sc.add(V, lambda e: e.tensor_tensor(
            out=RS1[:, 64:128], in0=RNKE[:, 64:128], in1=RNKE[:, 0:64],
            op=ALU.add), tag="r_p3b", risky=True)
        sc.add(V, lambda e: e.tensor_tensor(
            out=RNKE[:], in0=RS1[:], in1=MSKE[:], op=ALU.subtract),
            tag="r_exc", risky=True)
        sc.add(V, lambda e: e.tensor_scalar(
            out=M0E[:], in0=RNKE[:], scalar1=0.0, scalar2=None, op0=ALU.is_equal),
            tag="r_m0a", risky=True)
        sc.add(V, lambda e: e.tensor_tensor(
            out=M0E[:], in0=M0E[:], in1=MSKE[:], op=ALU.mult), tag="r_m0b",
            risky=True)
        sc.add(V, lambda e: e.tensor_scalar(
            out=M1E[:], in0=RNKE[:], scalar1=1.0, scalar2=None, op0=ALU.is_equal),
            tag="r_m1a", risky=True)
        r_m1 = sc.add(V, lambda e: e.tensor_tensor(
            out=M1E[:], in0=M1E[:], in1=MSKE[:], op=ALU.mult), tag="r_m1b",
            risky=True)
        pfx_mm = None
        wdeps0 = ps_w(["B6"], [r_msk, ld_ltri])
        for e_ in range(E):
            pfx_mm = sc.add(T, lambda e, e_=e_: e.matmul(
                PS_B[:, e_ * 16:(e_ + 1) * 16], LTRIS[:],
                MSKE[:, e_ * 16:(e_ + 1) * 16], start=True, stop=True),
                deps=wdeps0 if e_ == 0 else [], tag=f"r_pfx{e_}")
        ps_w_done(["B6"], pfx_mm)
        cnt_mm = sc.add(T, lambda e: e.matmul(
            PS_B[0:1, 512:640], ONESC[:], MSKE[:], start=True, stop=True),
            deps=ps_w(["B7"], [mz_onesc, r_msk]), tag="r_cnt")
        ps_w_done(["B7"], cnt_mm)
        pfx_cp = sc.add(A, lambda e: e.copy(PFXE[:], PS_B[:, 0:128]),
                        deps=ps_r(["B6"], [pfx_mm]), tag="r_pfxcp")
        ps_r_done(["B6"], pfx_cp)
        crow_cp = sc.add(A, lambda e: e.copy(RW0[0:1, 0:128], PS_B[0:1, 512:640]),
                         deps=ps_r(["B7"], [cnt_mm]), tag="r_crowcp")
        ps_r_done(["B7"], crow_cp)
        X0, X1, X2, X3, X4 = 0, 128, 256, 384, 512

        def blk(base, off, n):
            return bass.AP(RW0, base + off, [[1536, 1], [16, E], [1, n]])
        rp = sc.add(V, lambda e: e.tensor_copy(out=blk(X1, 0, 1), in_=blk(X0, 0, 1)),
                    deps=[crow_cp], tag="rg1a", risky=True)
        sc.add(V, lambda e: e.tensor_tensor(
            out=blk(X1, 1, 15), in0=blk(X0, 1, 15), in1=blk(X0, 0, 15),
            op=ALU.add), tag="rg1b", risky=True)
        sc.add(V, lambda e: e.tensor_copy(out=blk(X2, 0, 2), in_=blk(X1, 0, 2)),
               tag="rg2a", risky=True)
        sc.add(V, lambda e: e.tensor_tensor(
            out=blk(X2, 2, 14), in0=blk(X1, 2, 14), in1=blk(X1, 0, 14),
            op=ALU.add), tag="rg2b", risky=True)
        sc.add(V, lambda e: e.tensor_copy(out=blk(X3, 0, 4), in_=blk(X2, 0, 4)),
               tag="rg3a", risky=True)
        sc.add(V, lambda e: e.tensor_tensor(
            out=blk(X3, 4, 12), in0=blk(X2, 4, 12), in1=blk(X2, 0, 12),
            op=ALU.add), tag="rg3b", risky=True)
        sc.add(V, lambda e: e.tensor_copy(out=blk(X4, 0, 8), in_=blk(X3, 0, 8)),
               tag="rg4a", risky=True)
        sc.add(V, lambda e: e.tensor_tensor(
            out=blk(X4, 8, 8), in0=blk(X3, 8, 8), in1=blk(X3, 0, 8),
            op=ALU.add), tag="rg4b", risky=True)
        sc.add(V, lambda e: e.memset(blk(X1, 0, 1), 0.0), tag="rexc0", risky=True)
        rexc1_o = sc.add(V, lambda e: e.tensor_copy(
            out=blk(X1, 1, 15), in_=blk(X4, 0, 15)), tag="rexc1", risky=True)
        CNT = 640
        rcnt_o = sc.add(V, lambda e: e.tensor_copy(
            out=bass.AP(RW0, CNT, [[1536, 1], [1, E]]),
            in_=bass.AP(RW0, X4 + 15, [[1536, 1], [16, E]])), tag="rcnt",
            risky=True)
        rpcs_list = []
        for e_ in range(E):
            rpm = sc.add(V, lambda e, e_=e_: e.tensor_scalar(
                out=XN[0:1, e_ * 16:e_ * 16 + 16], in0=SIOS[0:1, 0:16],
                scalar1=RW0[0:1, 640 + e_:641 + e_], scalar2=None,
                op0=ALU.is_lt), deps=[ld_sio, rcnt_o], tag=f"rpcm{e_}",
                risky=True, chain=False)
            rpcs_list.append(sc.add(V, lambda e, e_=e_: e.reduce_sum(
                out=RW0[0:1, 672 + e_:673 + e_], in_=XN[0:1, e_ * 16:e_ * 16 + 16],
                axis=AX.X), deps=[rpm], tag=f"rpcs{e_}", risky=True,
                chain=False))
        rb_list = [sc.add(V, lambda e: e.memset(RW0[0:1, 680:681], 0.0),
                          tag="rb0", risky=True)]
        for e_ in range(E):
            rb_list.append(sc.add(V, lambda e, e_=e_: e.scalar_tensor_tensor(
                out=RW0[0:1, 681 + e_:682 + e_], in0=RW0[0:1, 672 + e_:673 + e_],
                scalar=128.0, in1=RW0[0:1, 680 + e_:681 + e_],
                op0=ALU.mult, op1=ALU.add), deps=[rpcs_list[e_]],
                tag=f"rb{e_ + 1}", risky=True))
        SLR, FTR, TMPR, MER = 704, 768, 384, 832
        for base, bo, opx in ((SLR, 681, ALU.is_ge), (FTR, 680, ALU.is_equal)):
            cmps = []
            for e_ in range(E):
                cmps.append(sc.add(V, lambda e, e_=e_, bo=bo, opx=opx:
                    e.tensor_scalar(
                        out=XN[0:1, 128 + e_ * 64:128 + e_ * 64 + 64],
                        in0=SIOS[:], scalar1=RW0[0:1, bo + e_:bo + e_ + 1],
                        scalar2=None, op0=opx),
                    deps=[ld_sio, rb_list[e_ + (1 if bo == 681 else 0)]],
                    tag=f"cm{base}_{e_}", risky=True, chain=False))
            lv1 = []
            for j in range(4):
                lv1.append(sc.add(V, lambda e, j=j: e.tensor_tensor(
                    out=XN[0:1, 640 + j * 64:640 + j * 64 + 64],
                    in0=XN[0:1, 128 + 2 * j * 64:128 + 2 * j * 64 + 64],
                    in1=XN[0:1, 128 + (2 * j + 1) * 64:128 + (2 * j + 1) * 64 + 64],
                    op=ALU.add), deps=[cmps[2 * j], cmps[2 * j + 1]],
                    tag=f"t1{base}_{j}", risky=True, chain=False))
            lv2 = []
            for j in range(2):
                lv2.append(sc.add(V, lambda e, j=j: e.tensor_tensor(
                    out=XN[0:1, 896 + j * 64:896 + j * 64 + 64],
                    in0=XN[0:1, 640 + 2 * j * 64:640 + 2 * j * 64 + 64],
                    in1=XN[0:1, 640 + (2 * j + 1) * 64:640 + (2 * j + 1) * 64 + 64],
                    op=ALU.add), deps=[lv1[2 * j], lv1[2 * j + 1]],
                    tag=f"t2{base}_{j}", risky=True, chain=False))
            sc.add(V, lambda e, base=base: e.tensor_tensor(
                out=RW0[0:1, base:base + 64], in0=XN[0:1, 896:960],
                in1=XN[0:1, 960:1024], op=ALU.add), deps=lv2,
                tag=f"t3{base}", risky=True)
        sc.add(V, lambda e: e.tensor_scalar(
            out=RW0[0:1, FTR:FTR + 64], in0=RW0[0:1, FTR:FTR + 64], scalar1=1.0,
            scalar2=None, op0=ALU.is_ge), tag="rftc", risky=True)
        for e_ in range(E):
            sc.add(V, lambda e, e_=e_: e.tensor_scalar(
                out=bass.AP(RW0, MER + e_ * NSLOT, [[1536, 1], [1, NSLOT]]),
                in0=RW0[0:1, SLR:SLR + NSLOT], scalar1=float(e_), scalar2=None,
                op0=ALU.is_equal), tag=f"rme{e_}", risky=True, chain=False)
        r_eb = sc.add(V, lambda e: e.tensor_scalar(
            out=RW0[0:1, SLR:SLR + 64], in0=RW0[0:1, SLR:SLR + 64], scalar1=128.0,
            scalar2=None, op0=ALU.mult), tag="r_eb", risky=True)
        r_obr = None
        for e_ in range(E):
            r_obr = sc.add(V, lambda e, e_=e_: e.tensor_scalar(
                out=RW0[0:1, X2 + e_ * 16:X2 + e_ * 16 + 16],
                in0=RW0[0:1, X1 + e_ * 16:X1 + e_ * 16 + 16],
                scalar1=RW0[0:1, 680 + e_:681 + e_], scalar2=None, op0=ALU.add),
                deps=[rexc1_o], tag=f"robr{e_}", risky=True)
        ob_mm = sc.add(T, lambda e: e.matmul(
            PS_H[0:128, 0:128], ONESR[:], RW0[0:1, X2:X2 + 128],
            start=True, stop=True),
            deps=ps_w(["H01"], [r_obr, mz_ones]), tag="r_obmm")
        ps_w_done(["H01"], ob_mm)
        ob_cp = sc.add(A, lambda e: e.copy(OFFBE[:], PS_H[:, 0:128]),
                       deps=ps_r(["H01"], [ob_mm]), tag="r_obcp")
        ps_r_done(["H01"], ob_cp)
        pk_mm = sc.add(T, lambda e: e.matmul(
            PS_H[0:128, 1024:1024 + 440], ONESR[:], RW0[0:1, SLR:SLR + 440],
            start=True, stop=True),
            deps=ps_w(["H23"], [r_eb]), tag="r_pkmm")
        ps_w_done(["H23"], pk_mm)
        pk_cp = sc.add(A, lambda e: e.copy(EBF[:, 0:128], PS_H[:, 1024:1152]),
                       deps=ps_r(["H23"], [pk_mm]), tag="r_pkcp")
        me_cp = sc.add(A, lambda e: e.copy(
            MASKE2[:], PS_H[:, 1152:1152 + 8 * NSLOT]),
            deps=[pk_cp], tag="r_mecp")
        ps_r_done(["H23"], me_cp)
        w_i1 = sc.add(V, lambda e: e.tensor_scalar(
            out=WIDXF[:, 64:128], in0=EBF[:, 0:64], scalar1=TIDTS[:, 0:1],
            scalar2=None, op0=ALU.add), deps=[pk_cp, ld_tidt], tag="w_i1",
            risky=True)
        w_o1 = sc.add(V, lambda e: e.tensor_scalar(
            out=FLD[:, 64:65], in0=TIDTS[:, 0:1], scalar1=1.0, scalar2=100000.0,
            op0=ALU.is_ge, op1=ALU.mult), deps=[ld_tidt], tag="w_o1", risky=True)
        w_i2 = sc.add(V, lambda e: e.tensor_scalar(
            out=FLD[:, 0:64], in0=EBF[:, 64:128], scalar1=-1.0,
            scalar2=1.0, op0=ALU.mult, op1=ALU.add), tag="w_i2", risky=True)
        w_i2b = sc.add(V, lambda e: e.tensor_scalar(
            out=FLD[:, 0:64], in0=FLD[:, 0:64], scalar1=FLD[:, 64:65],
            scalar2=None, op0=ALU.mult), tag="w_i2b", risky=True)
        w_cl = sc.add(V, lambda e: e.tensor_scalar(
            out=WIDXF[:, 64:128], in0=WIDXF[:, 64:128], scalar1=1023.0,
            scalar2=None, op0=ALU.min), tag="w_cl", risky=True)
        w_i3 = sc.add(V, lambda e: e.tensor_tensor(
            out=WIDXF[:, 0:64], in0=WIDXF[:, 64:128], in1=FLD[:, 0:64],
            op=ALU.add), tag="w_i3", risky=True)
        w_iu = sc.add(V, lambda e: e.tensor_copy(out=WIDXU[:], in_=WIDXF[:]),
                      tag="w_iu", risky=True)
        r_pose = sc.add(V, lambda e: e.tensor_tensor(
            out=POSE[:], in0=PFXE[:], in1=OFFBE[:], op=ALU.add),
            deps=[pfx_cp, ob_cp], tag="r_pose", risky=True)
        acc_fin = []
        uregs = (RS1, RNKE, MSKE, PFXE, OFFBE, EBF)
        ui = 0
        for k, MK in ((0, M0E), (1, M1E)):
            for qi, src_ in enumerate(("w", "pos", "e")):
                U = uregs[ui]
                ui += 1
                if src_ == "w":
                    op1 = sc.add(V, lambda e, MK=MK, U=U: e.tensor_tensor(
                        out=U[:].rearrange("p (e g) -> p e g", g=NT),
                        in0=MK[:].rearrange("p (e g) -> p e g", g=NT),
                        in1=walle, op=ALU.mult), deps=[r_m1],
                        tag=f"ac{k}{qi}a", risky=True, chain=False)
                elif src_ == "pos":
                    op1 = sc.add(V, lambda e, MK=MK, U=U: e.tensor_tensor(
                        out=U[:], in0=MK[:], in1=POSE[:], op=ALU.mult),
                        deps=[r_pose], tag=f"ac{k}{qi}a", risky=True,
                        chain=False)
                else:
                    op1 = sc.add(V, lambda e, MK=MK, U=U: e.tensor_tensor(
                        out=U[:], in0=MK[:], in1=EMAJS[:], op=ALU.mult),
                        deps=[ld_emaj], tag=f"ac{k}{qi}a", risky=True,
                        chain=False)
                op2 = sc.add(V, lambda e, U=U: e.tensor_tensor(
                    out=U[:, 0:64], in0=U[:, 0:64], in1=U[:, 64:128],
                    op=ALU.add), deps=[op1], tag=f"ac{k}{qi}b", risky=True,
                    chain=False)
                op3 = sc.add(V, lambda e, U=U: e.tensor_tensor(
                    out=U[:, 0:32], in0=U[:, 0:32], in1=U[:, 32:64],
                    op=ALU.add), deps=[op2], tag=f"ac{k}{qi}c", risky=True,
                    chain=False)
                dst = k * 64 + qi * 16
                acc_fin.append(sc.add(V, lambda e, dst=dst, U=U: e.tensor_tensor(
                    out=ACCS[:, dst:dst + 16], in0=U[:, 0:16],
                    in1=U[:, 16:32], op=ALU.add), deps=[op3],
                    tag=f"ac{k}{qi}d", risky=True, chain=False))
        r_sr1 = sc.add(V, lambda e: e.tensor_scalar(
            out=RS1[:, 0:16], in0=TIDTS[:], scalar1=2048.0, scalar2=None,
            op0=ALU.add), deps=[ld_tidt], tag="r_sr1", risky=True)
        rec_ops = []
        for k in range(2):
            for c, srcap in ((0, TIDTS[:]), (1, ACCS[:, k * 64:k * 64 + 16]),
                             (2, TIDTS[:] if k == 0 else RS1[:, 0:16]),
                             (3, ACCS[:, k * 64 + 32:k * 64 + 48])):
                rec_ops.append(sc.add(V, lambda e, k=k, c=c, srcap=srcap:
                    e.tensor_copy(
                        out=bass.AP(VREC, k * 64 + c, [[128, 128], [4, 16]]),
                        in_=srcap), deps=[r_sr1] + acc_fin,
                    tag=f"vr{k}{c}", risky=True, chain=False))
            rec_ops.append(sc.add(V, lambda e, k=k: e.tensor_copy(
                out=IRECU[:, k * 16:(k + 1) * 16],
                in_=ACCS[:, k * 64 + 16:k * 64 + 32]), deps=acc_fin,
                tag=f"ir{k}", risky=True, chain=False))
        scat0 = []
        for k in range(2):
            for g in range(NT):
                scat0.append(sc.add(GP, lambda e, k=k, g=g: e.indirect_dma_start(
                    out=slot_d.ap(), out_offset=IOA(
                        ap=IRECU[:, k * 16 + g:k * 16 + g + 1], axis=0),
                    in_=bass.AP(VREC, k * 64 + g * 4, [[128, 128], [1, 4]]),
                    in_offset=None),
                    deps=(def_dmas + rec_ops) if (k == 0 and g == 0) else [],
                    dma=True, tag=f"rsc{k}_{g}"))


        scat0_brg = gp_bridge(scat0, "scat0b")
        mk = {}
        scat_ops = []
        zd_all = p2st["zd_ops"]
        fr = {}

        def slot_front(s_):
            par = s_ % 2
            qr = s_ % 4
            met = sc.add(SY, lambda e, s_=s_, qr=qr: e.dma_start(
                out=SMET[:, qr * 4:(qr + 1) * 4],
                in_=slot_d.ap()[s_ * 128:(s_ + 1) * 128, :]),
                deps=[scat0_brg, mk.get(("stt", qr))] +
                     ([mk.get(("scat", 1 - par))] if SERIAL else []),
                dma=True, tag=f"met{s_}")
            cnv = sc.add(V, lambda e, qr=qr: e.tensor_copy(
                out=SIDU[:, qr * 2:qr * 2 + 1],
                in_=SMET[:, qr * 4:qr * 4 + 1]), deps=[met,
                mk.get(("zg", qr)), mk.get(("scat", qr))],
                tag=f"cnv{s_}a", risky=True)
            cnv2 = sc.add(V, lambda e, qr=qr: e.tensor_copy(
                out=SIDU[:, qr * 2 + 1:qr * 2 + 2],
                in_=SMET[:, qr * 4 + 2:qr * 4 + 3]), tag=f"cnv{s_}b",
                risky=True)
            zsl = 2048 + par * 1024
            zts = 6144 + par * 1024
            zg_ = sc.add(GP, lambda e, zsl=zsl, qr=qr: e.indirect_dma_start(
                out=MIX[:, zsl:zsl + 1024], out_offset=None,
                in_=zd_d.ap(), in_offset=IOA(
                    ap=SIDU[:, qr * 2:qr * 2 + 1], axis=0)),
                deps=[cnv, cnv2] + zd_all + [mk.get(("tr", par)),
                      mk.get(("wg2p", 0))],
                dma=True, tag=f"zg{s_}")
            mk[("zg", qr)] = zg_

            def tr_emit(e, zsl=zsl):
                for c in range(8):
                    last = e.transpose(PS_B[:, c * 128:(c + 1) * 128],
                                       MIX[:, zsl + c * 128:zsl + (c + 1) * 128],
                                       IDN[:])
                return last
            trs = sc.add(T, tr_emit,
                         deps=ps_w(["B6", "B7"], [zg_, idnb_cp]), tag=f"tr{s_}")
            ps_w_done(["B6", "B7"], trs)
            mk[("tr", par)] = trs
            evz = sc.add(A, lambda e, zts=zts: e.copy(
                S1[:, zts:zts + 1024], PS_B[:, 0:1024]),
                deps=ps_r(["B6", "B7"], [trs]) + [mk.get(("hid2", par))] +
                     ([aost["a_aev"]] if s_ < 2 else []),
                tag=f"evz{s_}")
            ps_r_done(["B6", "B7"], evz)
            fr[s_] = (zg_, trs, evz)

        def slot_back(s_):
            par = s_ % 2
            qr = s_ % 4
            zts = 6144 + par * 1024
            zg_, trs, evz = fr.pop(s_)
            gated = WG_MODE in (1, 3)
            wg1 = sc.add(GP, lambda e, s_=s_, gated=gated: e.indirect_dma_start(
                out=W1S[:], out_offset=None, in_=wd1_d.ap(),
                in_offset=IOA(ap=WIDXU[:, (s_ if gated else 64 + s_):
                                  (s_ if gated else 64 + s_) + 1], axis=0),
                bounds_check=(get_breg(e) if WG_MODE >= 1 else None),
                oob_is_err=(WG_MODE == 0)),
                deps=[w_iu, mk.get(("hid2", 0)), mk.get(("hid2", 1))] +
                     (pp_stores if s_ == 0 else []) +
                     ([zg_] if (WG_MODE == 3 and WG_QUIESCE) else []),
                dma=True, tag=f"wg1_{s_}")
            wg2 = sc.add(GP, lambda e, s_=s_, gated=gated: e.indirect_dma_start(
                out=W2S[:], out_offset=None, in_=wd2_d.ap(),
                in_offset=IOA(ap=WIDXU[:, (s_ if gated else 64 + s_):
                                  (s_ if gated else 64 + s_) + 1], axis=0),
                bounds_check=(get_breg(e) if WG_MODE >= 1 else None),
                oob_is_err=(WG_MODE == 0)),
                deps=[mk.get(("outs", 0)), mk.get(("outs", 1))] +
                     ([aost["a_aev"]] if s_ == 0 else []) +
                     ([zg_] if (WG_MODE == 3 and WG_QUIESCE) else []),
                dma=True, tag=f"wg2_{s_}")
            mk[("wg2p", 0)] = wg2
            b1p = None
            for e_ in range(E):
                if e_ == 0:
                    b1p = sc.add(V, lambda e, par=par, s_=s_: e.tensor_scalar(
                        out=B1SL[:, par * 16:(par + 1) * 16], in0=B1T[:, 0:16],
                        scalar1=MASKE2[:, s_:s_ + 1], scalar2=None,
                        op0=ALU.mult),
                        deps=[me_cp, b1_last, mk.get(("gelu", par))],
                        tag=f"b1s{s_}_0", risky=True)
                else:
                    b1p = sc.add(V, lambda e, e_=e_, par=par, s_=s_:
                        e.scalar_tensor_tensor(
                            out=B1SL[:, par * 16:(par + 1) * 16],
                            in0=B1T[:, e_ * 16:(e_ + 1) * 16],
                            scalar=MASKE2[:, e_ * NSLOT + s_:e_ * NSLOT + s_ + 1],
                            in1=B1SL[:, par * 16:(par + 1) * 16],
                            op0=ALU.mult, op1=ALU.add), tag=f"b1s{s_}_{e_}",
                        risky=True)

            def hid_emit(e, zts=zts, hclo=0, hchi=8):
                for hc in range(hclo, hchi):
                    for kc in range(8):
                        last = e.matmul(
                            PS_H[:, hc * 128:(hc + 1) * 128],
                            W1S[:, kc * 2048 + hc * 128: kc * 2048 + (hc + 1) * 128],
                            S1[:, zts + kc * 128: zts + (kc + 1) * 128],
                            start=(kc == 0), stop=(kc == 7))
                return last
            hid1 = sc.add(T, lambda e, zts=zts: hid_emit(e, zts, 0, 8),
                          deps=ps_w(["H01"], [wg1, evz]), tag=f"hid1_{s_}")
            ps_w_done(["H01"], hid1)
            hid2 = sc.add(T, lambda e, zts=zts: hid_emit(e, zts, 8, 16),
                          deps=ps_w(["H23"], []), tag=f"hid2_{s_}")
            ps_w_done(["H23"], hid2)
            mk[("hid2", par)] = hid2
            gl = None
            gelu_ops = []
            for hc in range(16):
                gl = sc.add(A, lambda e, hc=hc, par=par: e.activation(
                    out=S1[:, par * 2048 + hc * 128: par * 2048 + (hc + 1) * 128],
                    in_=PS_H[:, hc * 128:(hc + 1) * 128], func=AF.Gelu,
                    bias=B1SL[:, par * 16 + hc:par * 16 + hc + 1], scale=1.0),
                    deps=(ps_r(["H01"], [hid1, b1p, mk.get(("outs", par))] +
                               ([aost["a_aev"]] if s_ < 2 else []))
                          if hc == 0 else
                          (ps_r(["H23"], [hid2]) if hc == 8 else [])),
                    tag=f"gel{s_}_{hc}")
                gelu_ops.append(gl)
            ps_r_done(["H01"], gelu_ops[7])
            ps_r_done(["H23"], gelu_ops[15])
            mk[("gelu", par)] = gl

            mk[("bk", s_)] = (gl, wg2)

        def slot_back2(s_):
            par = s_ % 2
            qr = s_ % 4
            gl, wg2 = mk.pop(("bk", s_))

            def outs_emit(e, par=par):
                for n in range(2):
                    for hc in range(16):
                        last = e.matmul(
                            PS_A[:, n * 512:(n + 1) * 512],
                            S1[:, par * 2048 + hc * 128: par * 2048 + (hc + 1) * 128],
                            W2S[:, hc * 1024 + n * 512: hc * 1024 + (n + 1) * 512],
                            start=(hc == 0), stop=(hc == 15))
                return last
            outs = sc.add(T, outs_emit,
                          deps=ps_w(["A4", "A5"], [gl, wg2]), tag=f"outs{s_}")
            ps_w_done(["A4", "A5"], outs)
            mk[("outs", par)] = outs
            stt = sc.add(A, lambda e, par=par, qr=qr: e.activation(
                out=MIX[:, par * 1024:(par + 1) * 1024], in_=PS_A[:, 0:1024],
                func=AF.Identity, scale=SMET[:, qr * 4 + 1:qr * 4 + 2]),
                deps=ps_r(["A4", "A5"], [outs]) + [mk.get(("scat", par))],
                tag=f"stt{s_}")
            ps_r_done(["A4", "A5"], stt)
            mk[("stt", qr)] = stt
            sca = sc.add(SY, lambda e, par=par, s_=s_: e.dma_start(
                out=abd_d.ap()[s_ * 128:(s_ + 1) * 128, :],
                in_=MIX[:, par * 1024:(par + 1) * 1024]),
                deps=[stt], dma=True, tag=f"sca{s_}")
            mk[("scat", par)] = sca
            mk[("scat", qr)] = sca
            scat_ops.append(sca)


        slot_front(0)
        for s_ in range(NSLOT):
            slot_back(s_)
            if s_ + 1 < NSLOT:
                slot_front(s_ + 1)
            slot_back2(s_)

        scat_brg = gp_bridge(scat_ops, "scatb")
        if dbg:
            sc.add(GP, lambda e: e.dma_start(out=dbg_abd.ap(), in_=abd_d.ap()),
                   deps=[scat_brg], dma=True, tag="dbg_abd")
            sc.add(GP, lambda e: e.dma_start(out=dbg_slotd.ap(), in_=slot_d.ap()),
                   deps=[scat_brg], dma=True, tag="dbg_slotd")
        epi_ld = {}
        ch_eout = {}
        ch_es4 = {}

        def epi_stage_ops(g):
            par = g % 2
            aoff = 2048 * (1 - par)
            XNl = XN if par == 0 else SQX
            TMPl = TMP if par == 0 else G3B
            Gl = ZF if par == 0 else B3B
            st = {}

            def s_dma():
                st["xl"] = sc.add(SY, lambda e, g=g, par=par: e.dma_start(
                    out=XT[par][:], in_=x_d.ap()[g * 128:(g + 1) * 128, :]),
                    deps=[last_xt_user[par]], dma=True, tag=f"xepi{g}")
                st["la"] = sc.add(GP, lambda e, g=g, aoff=aoff:
                    e.indirect_dma_start(
                        out=MIX[:, aoff:aoff + 1024], out_offset=None,
                        in_=abd_d.ap(),
                        in_offset=IOA(ap=IRECU[:, g:g + 1], axis=0)),
                    deps=[scat_brg, epi_ld.get(par)], dma=True, tag=f"la{g}")
                st["lb"] = sc.add(GP, lambda e, g=g, aoff=aoff:
                    e.indirect_dma_start(
                        out=MIX[:, aoff + 1024:aoff + 2048], out_offset=None,
                        in_=abd_d.ap(),
                        in_offset=IOA(ap=IRECU[:, 16 + g:16 + g + 1], axis=0)),
                    deps=[], dma=True, tag=f"lb{g}")

            def s_ebmm():
                rot = g % 4
                epair = (["B6", "B7"], ["A4", "A5"], ["H01"], ["H23"])[rot]

                def eslice(n0, n1):
                    if rot == 0:
                        return PS_B[:, n0:n1]
                    if rot == 1:
                        return PS_A[:, n0:n1]
                    if rot == 2:
                        return PS_H[:, n0:n1]
                    return PS_H[:, 1024 + n0:1024 + n1]

                def ebmm_emit(e, g=g, eslice=eslice):
                    for n in range(2):
                        last = e.matmul(
                            eslice(n * 512, (n + 1) * 512),
                            WTRALL[0:8, g * 128:(g + 1) * 128],
                            B2S[:, n * 512:(n + 1) * 512], start=True, stop=True)
                    return last
                st["ebmm"] = sc.add(T, ebmm_emit,
                                    deps=ps_w(epair, [wtr_cp[g], ld_b2s]),
                                    tag=f"ebmm{g}")
                ps_w_done(epair, st["ebmm"])
                st["epair"] = epair
                st["epap"] = eslice(0, 1024)

            def s_exn():
                st["exn"] = sc.add(A, lambda e, g=g, par=par, XNl=XNl:
                    e.activation(
                        out=XNl[:], in_=XT[par][:], func=AF.Identity,
                        scale=RSTD1[:, g:g + 1], bias=NMR[:, g:g + 1]),
                    deps=[st["xl"], b_nmr1, ch_eout.get(par)], tag=f"exn{g}")

            def s_ef0():
                st["ef0"] = sc.add(V, lambda e, aoff=aoff, TMPl=TMPl:
                    e.tensor_tensor(
                        out=TMPl[:], in0=MIX[:, aoff:aoff + 1024],
                        in1=MIX[:, aoff + 1024:aoff + 2048], op=ALU.add),
                    deps=[st["la"], st["lb"]], tag=f"ef0_{g}")
                epi_ld[par] = st["ef0"]

            def s_ef1():
                st["ef1"] = sc.add(V, lambda e, TMPl=TMPl: e.tensor_tensor(
                    out=TMPl[:], in0=TMPl[:], in1=st["epap"], op=ALU.add),
                    deps=ps_r(st["epair"], [st["ebmm"]]), tag=f"ef1_{g}")
                ps_r_done(st["epair"], st["ef1"])

            def s_ef2():
                st["ef2"] = sc.add(V, lambda e, TMPl=TMPl, XNl=XNl:
                    e.tensor_tensor(
                        out=TMPl[:], in0=TMPl[:], in1=XNl[:], op=ALU.add),
                    deps=[st["exn"]], tag=f"ef2_{g}")

            def s_eg():
                st["eg"] = sc.add(A, lambda e, TMPl=TMPl, Gl=Gl: e.activation(
                    out=Gl[:], in_=TMPl[:], func=AF.Silu),
                    deps=[st["ef2"], ch_es4.get(par)], tag=f"eg{g}")

            def s_ea():
                if g == 15:
                    st["ea"] = sc.add(V, lambda e, XNl=XNl: e.tensor_tensor(
                        out=XNl[:], in0=XNl[:], in1=APAD[:], op=ALU.add),
                        deps=[aost["apad_dma"], st["exn"]], tag=f"ea{g}")

            def s_es1():
                st["es1"] = sc.add(A, lambda e, g=g, XNl=XNl, TMPl=TMPl:
                    e.activation(
                        out=TMPl[:], in_=XNl[:], func=AF.Identity,
                        scale=RSTD2[:, g:g + 1], bias=NMR[:, NT + g:NT + g + 1]),
                    deps=[b_nmr2, st["eg"]] +
                         ([st["ea"], aost["nmr2f"]] if g == 15 else []),
                    tag=f"es1_{g}")

            def s_tailv():
                sc.add(V, lambda e, XNl=XNl, TMPl=TMPl: e.tensor_tensor(
                    out=XNl[:], in0=TMPl[:], in1=G2B[:], op=ALU.mult),
                    deps=[bcast_cp[2], st["es1"]], tag=f"es2_{g}")
                sc.add(V, lambda e, XNl=XNl, TMPl=TMPl: e.tensor_tensor(
                    out=TMPl[:], in0=XNl[:], in1=B2LB[:], op=ALU.add),
                    deps=[bcast_cp[3]], tag=f"es3_{g}")
                es4 = sc.add(V, lambda e, XNl=XNl, TMPl=TMPl, Gl=Gl:
                    e.tensor_tensor(
                        out=XNl[:], in0=TMPl[:], in1=Gl[:], op=ALU.mult),
                    deps=[st["eg"]], tag=f"es4_{g}")
                ch_es4[par] = es4
                epo = sc.add(V, lambda e, par=par, XNl=XNl: e.tensor_tensor(
                    out=OUTB[:, par * DM:(par + 1) * DM], in0=XNl[:],
                    in1=XT[par][:], op=ALU.add),
                    deps=[st["xl"]] + ([st_ops[g - 2]] if g >= 2 else []),
                    tag=f"eout{g}")
                ch_eout[par] = epo
                last_xt_user[par] = epo
                st_ops[g] = sc.add(SY, lambda e, g=g, par=par: e.dma_start(
                    out=y_d.ap()[g * 128:(g + 1) * 128, :],
                    in_=OUTB[:, par * DM:(par + 1) * DM]),
                    deps=[epo], dma=True, tag=f"st{g}")

            return [s_dma, s_ebmm, s_exn, s_ef0, s_ef1, s_ef2, s_eg, s_ea,
                    s_es1, s_tailv]

        for g0 in range(0, NT, 2):
            stages0 = epi_stage_ops(g0)
            stages1 = epi_stage_ops(g0 + 1)
            for f0, f1 in zip(stages0, stages1):
                f0()
                f1()

        sc.add(SY, lambda e: None,
               deps=[st_ops[g] for g in range(NT) if st_ops[g] is not None],
               tag="fin")

        sc.emit(block, sems)

    return nc


_NC_CACHE = None


def _get_nc():
    global _NC_CACHE
    if _NC_CACHE is None:
        _NC_CACHE = build_nc()
    return _NC_CACHE


def kernel(L0_attn_x, L0_attn_W_Q, L0_attn_W_K, L0_attn_W_V, L0_attn_W_Kp,
           L0_attn_W_O, L0_attn_ln_gamma, L0_attn_ln_beta, L0_ffn_ln_g,
           L0_ffn_ln_b, L0_ffn_rW, L0_ffn_rb, L0_ffn_W1, L0_ffn_b1,
           L0_ffn_W2, L0_ffn_b2):
    def f32(a):
        return np.ascontiguousarray(np.asarray(a, dtype=np.float32))

    x = f32(L0_attn_x)
    shared = {
        "wq": f32(L0_attn_W_Q), "wk": f32(L0_attn_W_K), "wkp": f32(L0_attn_W_Kp),
        "wv": f32(L0_attn_W_V), "wo": f32(L0_attn_W_O),
        "g2": f32(L0_attn_ln_gamma), "be2": f32(L0_attn_ln_beta),
        "g3": f32(L0_ffn_ln_g), "b3": f32(L0_ffn_ln_b),
        "rw": f32(L0_ffn_rW), "rb": f32(L0_ffn_rb),
        "w1": f32(L0_ffn_W1), "b1": f32(L0_ffn_b1),
        "w2": f32(L0_ffn_W2), "b2": f32(L0_ffn_b2),
    }
    t = np.arange(WIN)[:, None, None]
    j = np.arange(WIN)[None, :, None]
    k = np.arange(WIN)[None, None, :]
    mask = np.where((j <= t) & (k <= t), 0.0, NEG_BIG).astype(np.float32)
    shared["mask"] = np.ascontiguousarray(mask.reshape(WIN, WIN * WIN))
    shared["ident"] = np.eye(128, dtype=np.float32)
    shared["ltri"] = np.triu(np.ones((128, 128), np.float32), 1)
    shared["tidt"] = np.ascontiguousarray(
        (np.arange(NT)[None, :] * 128 + np.arange(128)[:, None]).astype(np.float32))
    emaj = np.zeros((128, 128), np.float32)
    for e in range(E):
        emaj[:, e * NT:(e + 1) * NT] = e
    shared["emaj"] = emaj
    d4 = np.zeros((128, 4), np.float32)
    d4[:, 2] = 4096 + np.arange(128)
    shared["def4"] = d4
    sio = np.full((1, 64), 1e9, np.float32)
    sio[0, :NSLOT] = np.arange(NSLOT) * 128
    shared["sio"] = sio

    nc = _get_nc()
    in_maps = []
    for b in range(8):
        m = dict(shared)
        m["x"] = np.ascontiguousarray(x[b])
        in_maps.append(m)
    res = run_bass_kernel_spmd(nc, in_maps, core_ids=list(range(8)))
    return np.stack([r["y"] for r in res.results], axis=0)












# version 43
# speedup vs baseline: 1.0522x; 1.0522x over previous
import numpy as np
import os
WG_SKIP = os.environ.get("WG_SKIP", "1") == "1"
SERIAL = os.environ.get("SERIAL", "0") == "1"
WG_MODE = int(os.environ.get("WG_MODE", "3"))
WG_QUIESCE = os.environ.get("WG_Q", "1") == "1"
from contextlib import ExitStack

import concourse.bass as bass
from bass_rust import IndirectOffsetOnAxis as IOA
import concourse.mybir as mybir
from concourse.bass_utils import run_bass_kernel_spmd

F32 = mybir.dt.float32
U32 = mybir.dt.uint32
BF16 = mybir.dt.bfloat16
AF = mybir.ActivationFunctionType
ALU = mybir.AluOpType
AX = mybir.AxisListType

S = 2048
DM = 1024
NT = 16
H = 16
DH = 64
WIN = 32
NSLOT = 39
E = 8
HF = 2048
EPS_LN = 1e-5
EPS_NORM = 1e-7
NEG_BIG = -1.0e30
I1K = 1.0 / DM


class Op:
    __slots__ = ("i", "eng", "emit", "deps", "dma", "tag", "need_inc", "val",
                 "risky", "chain", "qwait")

    def __init__(self, i, eng, emit, deps, dma, tag, risky=False):
        self.i, self.eng, self.emit, self.deps = i, eng, emit, deps
        self.dma, self.tag = dma, tag
        self.need_inc = False
        self.val = None
        self.risky = risky
        self.chain = True
        self.qwait = None


class Sched:
    def __init__(self):
        self.ops = []

    def add(self, eng, emit, deps=(), dma=False, tag="", risky=False,
            chain=True):
        deps = [d for d in deps if d is not None]
        o = Op(len(self.ops), eng, emit, deps, dma, tag, risky)
        o.chain = chain
        for d in deps:
            assert d.i < o.i, f"forward dep {d.tag} -> {o.tag}"
        self.ops.append(o)
        return o

    def emit(self, block, sems):
        pending = {}
        for o in self.ops:
            if o.dma:
                continue
            p = pending.get(o.eng)
            if p is not None:
                o.deps.append(p)
                pending[o.eng] = None
            if o.risky:
                o.need_inc = True
                if o.chain:
                    pending[o.eng] = o
        for o in self.ops:
            if o.dma:
                o.need_inc = True
            for d in o.deps:
                if d.dma or d.eng != o.eng:
                    d.need_inc = True
        counts = {}
        for o in self.ops:
            if o.need_inc:
                key = ("dma", o.eng) if o.dma else o.eng
                counts[key] = counts.get(key, 0) + 1
                o.val = counts[key] * (16 if o.dma else 1)

        def semkey(o):
            return ("dma", o.eng) if o.dma else o.eng

        obs = {}

        def dep_val(d):
            return obs[d.i] if d.dma else d.val

        for ename in ("sync", "gpsimd"):
            waited = {}
            batch = []

            def close():
                if batch:
                    end = batch[-1].val
                    for b in batch:
                        obs[b.i] = end
                    batch.clear()
                    return end
                return None

            for o in self.ops:
                if o.eng != ename:
                    continue
                need_close = any(d.dma and d.eng == ename and d.i not in obs
                                 for d in o.deps)
                emits = False
                for d in o.deps:
                    if not d.dma and d.eng == ename and not d.risky:
                        continue
                    k = semkey(d)
                    if d.dma and d.eng == ename and d.i not in obs:
                        continue
                    if d.dma and d.eng != ename:
                        assert d.i in obs, (
                            f"dma dep {d.tag} of {o.tag} not yet batched "
                            f"(cross-queue open batch?)")
                    v = dep_val(d)
                    if waited.get(k, -1) >= v:
                        continue
                    emits = True
                    waited[k] = v
                if emits or need_close:
                    end = close()
                    if end is not None and ename == "sync":
                        o.qwait = end
                        waited[("dma", ename)] = max(
                            waited.get(("dma", ename), -1), end)
                if need_close:
                    for d in o.deps:
                        if d.dma and d.eng == ename:
                            k = semkey(d)
                            v = dep_val(d)
                            if waited.get(k, -1) < v:
                                waited[k] = v
                if o.dma:
                    batch.append(o)
            close()

        def run_engine(eng_obj, eng_name):
            waited = {}
            for o in self.ops:
                if o.eng != eng_name:
                    continue
                if o.qwait is not None:
                    qk = ("dma", eng_name)
                    if waited.get(qk, -1) < o.qwait:
                        eng_obj.wait_ge(sems[qk], o.qwait)
                        waited[qk] = o.qwait
                for d in o.deps:
                    if not d.dma and d.eng == eng_name and not d.risky:
                        continue
                    k = semkey(d)
                    v = dep_val(d)
                    assert v is not None, f"dep {d.tag} of {o.tag} has no val"
                    if waited.get(k, -1) >= v:
                        continue
                    eng_obj.wait_ge(sems[k], v)
                    waited[k] = v
                inst = o.emit(eng_obj)
                if o.need_inc:
                    assert inst is not None, f"op {o.tag} needs inc, emitted nothing"
                    inst.then_inc(sems[semkey(o)], 16 if o.dma else 1)

        @block.tensor
        def _(e):
            run_engine(e, "tensor")

        @block.vector
        def _(e):
            run_engine(e, "vector")

        @block.scalar
        def _(e):
            run_engine(e, "scalar")

        @block.sync
        def _(e):
            run_engine(e, "sync")

        @block.gpsimd
        def _(e):
            run_engine(e, "gpsimd")


def build_nc(phase="full", cut=99, dbg=False):
    nc = bass.Bass("TRN2", target_bir_lowering=False)

    x_d = nc.dram_tensor("x", [S, DM], F32, kind="ExternalInput")
    wq_d = nc.dram_tensor("wq", [DM, DM], F32, kind="ExternalInput")
    wk_d = nc.dram_tensor("wk", [DM, DM], F32, kind="ExternalInput")
    wkp_d = nc.dram_tensor("wkp", [DM, DM], F32, kind="ExternalInput")
    wv_d = nc.dram_tensor("wv", [DM, DM], F32, kind="ExternalInput")
    wo_d = nc.dram_tensor("wo", [DM, DM], F32, kind="ExternalInput")
    g2_d = nc.dram_tensor("g2", [DM], F32, kind="ExternalInput")
    be2_d = nc.dram_tensor("be2", [DM], F32, kind="ExternalInput")
    g3_d = nc.dram_tensor("g3", [DM], F32, kind="ExternalInput")
    b3_d = nc.dram_tensor("b3", [DM], F32, kind="ExternalInput")
    rw_d = nc.dram_tensor("rw", [DM, E], F32, kind="ExternalInput")
    rb_d = nc.dram_tensor("rb", [E], F32, kind="ExternalInput")
    w1_d = nc.dram_tensor("w1", [E, DM, HF], F32, kind="ExternalInput")
    b1_d = nc.dram_tensor("b1", [E, HF], F32, kind="ExternalInput")
    w2_d = nc.dram_tensor("w2", [E, HF, DM], F32, kind="ExternalInput")
    b2_d = nc.dram_tensor("b2", [E, DM], F32, kind="ExternalInput")
    mask_d = nc.dram_tensor("mask", [WIN, WIN * WIN], F32, kind="ExternalInput")
    ident_d = nc.dram_tensor("ident", [128, 128], F32, kind="ExternalInput")
    ltri_d = nc.dram_tensor("ltri", [128, 128], F32, kind="ExternalInput")
    tidt_d = nc.dram_tensor("tidt", [128, NT], F32, kind="ExternalInput")
    emaj_d = nc.dram_tensor("emaj", [128, 128], F32, kind="ExternalInput")
    def4_d = nc.dram_tensor("def4", [128, 4], F32, kind="ExternalInput")
    sio_d = nc.dram_tensor("sio", [1, 64], F32, kind="ExternalInput")
    wd1_d = nc.dram_tensor("wd1", [E * 128, 16 * 1024], BF16, kind="Internal")
    wd2_d = nc.dram_tensor("wd2", [E * 128, 16 * 1024], BF16, kind="Internal")
    zd_d = nc.dram_tensor("zd", [S, DM], F32, kind="Internal")
    slot_d = nc.dram_tensor("slotd", [NSLOT * 128, 4], F32, kind="Internal")
    abd_d = nc.dram_tensor("abd", [NSLOT * 128, DM], F32, kind="Internal")
    y_d = nc.dram_tensor("y", [S, DM], F32, kind="ExternalOutput")
    if dbg:
        dbg_wall = nc.dram_tensor("dbg_wall", [128, NT * E], F32, kind="ExternalOutput")
        dbg_stats = nc.dram_tensor("dbg_stats", [128, 4 * NT], F32, kind="ExternalOutput")
        dbg_abd = nc.dram_tensor("dbg_abd", [4352, DM], F32, kind="ExternalOutput")
        dbg_slotd = nc.dram_tensor("dbg_slotd", [NSLOT * 128, 4], F32, kind="ExternalOutput")
        dbg_zt = nc.dram_tensor("dbg_zt", [128, 8 * S], mybir.dt.bfloat16, kind="ExternalOutput")
        dbg_apad = nc.dram_tensor("dbg_apad", [128, DM], F32, kind="ExternalOutput")
        dbg_ztb = nc.dram_tensor("dbg_ztb", [128, 8 * WIN], mybir.dt.bfloat16, kind="ExternalOutput")
        dbg_mix = nc.dram_tensor("dbg_mix", [128, 8 * DM], F32, kind="ExternalOutput")

    with ExitStack() as ctx:
        def sb(name, shape, dtype=F32, c=None):
            return (c or ctx).enter_context(nc.sbuf_tensor(name, shape, dtype))

        attx = ExitStack()
        IDN = sb("idn", [128, 128])
        ONESR = sb("onesr", [1, 128])
        G2B = sb("g2b", [128, DM])
        B2LB = sb("b2lb", [128, DM])
        G3B = sb("g3b", [128, DM])
        B3B = sb("b3b", [128, DM])
        RW = sb("rw_sb", [128, 64])
        RB = sb("rb_sb", [E, 1])
        B1SA = sb("b1sa", [128, 128])
        B1T = sb("b1t", [128, 128])
        B2S = sb("b2s", [E, DM], BF16)
        MU1 = sb("mu1", [128, NT]); RSTD1 = sb("rstd1", [128, NT])
        MU2 = sb("mu2", [128, NT]); RSTD2 = sb("rstd2", [128, NT])
        WALL = sb("wall", [128, NT * E])
        XT = [sb("xt0", [128, DM]), sb("xt1", [128, DM])]
        XN = sb("xn", [128, DM])
        SQX = sb("sqx", [128, DM])
        TMP = sb("tmp", [128, DM])
        ZF = sb("zf", [128, DM])
        ZTF = sb("ztf", [128, DM])
        SCR = sb("scr", [128, 8])
        LGT = sb("lgt", [E, 128])
        RBR = sb("rbr", [1, E])
        S1T = sb("s1t", [128, NT]); S2T = sb("s2t", [128, NT])
        RT = sb("rt", [128, NT])
        NMR = sb("nmr", [128, 2 * NT])
        WTRALL = sb("wtrall", [E, NT * 128], BF16)
        LTE = sb("lte", [128, NT * E])
        EXB = sb("exb", [128, NT * E])
        AUX = sb("aux", [128, NT * E])
        TRE = sb("tre", [128, 128])
        APAD = sb("apad", [128, DM])
        MIX = sb("mix", [128, 4 * DM])
        OUTB = sb("outb", [128, 2 * DM])
        S0 = sb("s0", [128, 8 * 1024], BF16)
        S1 = sb("s1", [128, 8 * 1024], BF16)
        MASK = sb("maskt", [WIN, WIN * WIN], c=attx)
        QT = sb("qt", [128, 8 * WIN], BF16, c=attx)
        KT = sb("kt", [128, 8 * WIN], c=attx)
        KPT = sb("kpt", [128, 8 * WIN], c=attx)
        VT = sb("vt", [128, 8 * WIN], c=attx)
        MT = sb("mt", [128, WIN * WIN], BF16, c=attx)
        NTJ = sb("ntj", [128, 8 * DH], BF16, c=attx)
        PF = sb("pf", [WIN, WIN * WIN], c=attx)
        PSC = sb("psc", [WIN, WIN * WIN], c=attx)
        PT = sb("pt", [128, 8 * WIN], BF16, c=attx)
        ZTB = sb("ztb", [128, 8 * WIN], BF16, c=attx)
        PSR = sb("psr", [WIN, 1], c=attx)
        AOUT = sb("aout", [WIN, DM], c=attx)
        XWIN = sb("xwin", [WIN, DM], c=attx)

        PS_H = ctx.enter_context(nc.psum_tensor("ps_h", [128, 2048], F32))
        PS_A = ctx.enter_context(nc.psum_tensor("ps_a", [128, 1024], F32))
        PS_B = ctx.enter_context(nc.psum_tensor("ps_b", [128, 1024], F32))

        sems = {}
        for e in ("tensor", "vector", "scalar", "sync", "gpsimd"):
            sems[e] = ctx.enter_context(nc.semaphore(f"sem_{e}"))
        for e in ("sync", "gpsimd"):
            sems[("dma", e)] = ctx.enter_context(nc.semaphore(f"dsem_{e}"))
        block = ctx.enter_context(nc.Block())

        sc = Sched()
        V, A, T, SY, GP = "vector", "scalar", "tensor", "sync", "gpsimd"

        slots = {k: {"w": None, "r": []} for k in
                 ("H01", "H23", "A4", "A5", "B6", "B7")}

        def ps_w(names, op_deps):
            dd = list(op_deps)
            for n in names:
                dd += slots[n]["r"]
            return dd

        def ps_w_done(names, op):
            for n in names:
                slots[n]["w"] = op
                slots[n]["r"] = []

        def ps_r(names, op_deps):
            dd = list(op_deps)
            for n in names:
                if slots[n]["w"] is not None:
                    dd.append(slots[n]["w"])
            return dd

        def ps_r_done(names, op):
            for n in names:
                slots[n]["r"].append(op)

        ld_b2s = sc.add(GP, lambda e: e.dma_start(out=B2S[:], in_=b2_d.ap()),
                        dma=True, tag="ld_b2s")
        mz_ones = sc.add(V, lambda e: e.memset(ONESR[:], 1.0), tag="ones")
        mz_apad = sc.add(V, lambda e: e.memset(APAD[:], 0.0), tag="apad0")

        bc_stage = [XN[0:1, :], MIX[0:1, 0:DM], OUTB[0:1, 0:DM],
                    OUTB[0:1, DM:2 * DM]]
        bcast_cp = []
        for i, (d, dst) in enumerate(((g3_d, G3B), (b3_d, B3B), (g2_d, G2B), (be2_d, B2LB))):
            stg = bc_stage[i]
            ldp = sc.add(SY, lambda e, d=d, stg=stg: e.dma_start(
                out=stg, in_=bass.AP(d, 0, [[DM, 1], [1, DM]])),
                dma=True, tag=f"ld_prm{i}")

            def bcmm(e, stg=stg):
                for n in range(2):
                    last = e.matmul(PS_A[:, n * 512:(n + 1) * 512], ONESR[:],
                             stg[:, n * 512:(n + 1) * 512], start=True, stop=True)
                return last
            m = sc.add(T, bcmm, deps=ps_w(["A4", "A5"], [ldp, mz_ones]), tag=f"bc_mm{i}")
            ps_w_done(["A4", "A5"], m)
            cpo = sc.add(V, lambda e, dst=dst: e.tensor_copy(out=dst[:], in_=PS_A[:]),
                         deps=ps_r(["A4", "A5"], [m]), tag=f"bc_cp{i}")
            ps_r_done(["A4", "A5"], cpo)
            bcast_cp.append(cpo)

        def load_w_1024(slot, src_d, tag, deps):
            def emit(e):
                return e.dma_start(
                    out=slot[:].rearrange("p (kc j) -> p kc j", kc=8),
                    in_=src_d.ap().rearrange("(kc p) j -> p kc j", p=128))
            return sc.add(GP, emit, deps=deps, dma=True, tag=tag)


        def _finalize_partial():
            lasts = {}
            for o in sc.ops:
                lasts[o.eng] = o
            sc.add(SY, lambda e: None,
                   deps=[v for k, v in lasts.items() if k != SY] +
                        ([lasts[SY]] if lasts.get(SY) is not None and lasts[SY].dma else []),
                   tag="fin_partial")
            sc.emit(block, sems)

        if phase == "p0":
            _finalize_partial()
            return nc

        t_xn = [None] * NT
        zc_ops = [[None] * 8 for _ in range(NT)]
        t_w = [None] * NT
        last_xt_user = [None, None]
        last_ztr = None
        xwin_dma = None

        last_red = None
        p1_bufs = [XT[0], XT[1], TMP, ZF]
        p1_user = [None, None, None, None]
        for g in range(NT):
            bi = g % 4
            xt = p1_bufs[bi]
            xl = sc.add(SY if g % 2 == 0 else GP,
                lambda e, g=g, xt=xt: e.dma_start(
                    out=xt[:], in_=x_d.ap()[g * 128:(g + 1) * 128, :]),
                deps=[p1_user[bi]], dma=True, tag=f"xla{g}")
            r1 = sc.add(A, lambda e, xt=xt, g=g: e.activation(
                out=SQX[:], in_=xt[:], func=AF.Identity,
                accum_out=S1T[:, g:g + 1]),
                deps=[xl], tag=f"p1s{g}", risky=True)
            r2 = sc.add(A, lambda e, xt=xt, g=g: e.activation(
                out=SQX[:], in_=xt[:], func=AF.Square,
                accum_out=S2T[:, g:g + 1]),
                tag=f"p1ss{g}", risky=True)
            p1_user[bi] = r2
            last_red = r2
        last_xt_user[0] = p1_user[0]
        last_xt_user[1] = p1_user[1]

        ld_idn = sc.add(SY, lambda e: e.dma_start(out=IDN[:], in_=ident_d.ap()),
                        dma=True, tag="ld_idn")
        ld_rw = sc.add(SY, lambda e: e.dma_start(
            out=RW[:].rearrange("p (kc e) -> p kc e", e=E),
            in_=rw_d.ap().rearrange("(kc p) e -> p kc e", p=128)),
            dma=True, tag="ld_rw")
        ld_rb = sc.add(SY, lambda e: e.dma_start(
            out=RB[:], in_=bass.AP(rb_d, 0, [[1, E], [1, 1]])), dma=True, tag="ld_rb")
        ld_rbr = sc.add(SY, lambda e: e.dma_start(
            out=RBR[:], in_=bass.AP(rb_d, 0, [[E, 1], [1, E]])), dma=True, tag="ld_rbr")
        ldb1 = sc.add(SY, lambda e: e.dma_start(
            out=B1SA[:], in_=bass.AP(b1_d, 0, [[128, 128], [1, 128]])),
            dma=True, tag="ld_b1")
        trb1 = sc.add(T, lambda e: e.transpose(PS_B[:, 0:128], B1SA[:], IDN[:]),
                      deps=ps_w(["B6"], [ldb1, ld_idn]), tag="tr_b1")
        ps_w_done(["B6"], trb1)
        b1_last = sc.add(V, lambda e: e.tensor_copy(out=B1T[:], in_=PS_B[:, 0:128]),
                         deps=ps_r(["B6"], [trb1]), tag="cp_b1")
        ps_r_done(["B6"], b1_last)
        ld_wq = load_w_1024(S0, wq_d, "ld_wq", [p1_user[3]])
        ld_wk = load_w_1024(S1, wk_d, "ld_wk", [])

        pp_stores = []

        def emit_pp(lo, hi, deps0=()):
            i = 0
            for e_ in range(E):
                for ti in range(2):
                    for hf in range(2):
                        if not (lo <= i < hi):
                            i += 1
                            continue
                        if ti == 0:
                            inap = bass.AP(w1_d, e_ * DM * HF + hf * 4 * 128 * HF,
                                           [[HF, 128], [128 * HF, 4], [1, HF]])
                            oap = bass.AP(wd1_d, e_ * 128 * 16384 + hf * 8192,
                                          [[16384, 128], [2048, 4], [1, 2048]])
                        else:
                            inap = bass.AP(w2_d, e_ * HF * DM + hf * 8 * 128 * DM,
                                           [[DM, 128], [128 * DM, 8], [1, DM]])
                            oap = bass.AP(wd2_d, e_ * 128 * 16384 + hf * 8192,
                                          [[16384, 128], [1024, 8], [1, 1024]])
                        pp_stores.append(sc.add(
                            GP, lambda e, inap=inap, oap=oap: e.dma_start(
                                out=oap, in_=inap),
                            deps=(list(deps0) if i == lo else []),
                            dma=True, tag=f"pp{i}"))
                        i += 1


        emit_pp(0, 2, [p1_user[2]])

        sc.add(V, lambda e: e.tensor_scalar(
            out=MU1[:], in0=S1T[:], scalar1=I1K, scalar2=None, op0=ALU.mult),
            deps=[last_red], tag="b_mu1", risky=True)
        sc.add(V, lambda e: e.tensor_scalar(
            out=TRE[:, 0:16], in0=S2T[:], scalar1=I1K, scalar2=None, op0=ALU.mult),
            tag="b_ex2", risky=True)
        sc.add(V, lambda e: e.tensor_tensor(
            out=TRE[:, 16:32], in0=MU1[:], in1=MU1[:], op=ALU.mult),
            tag="b_musq", risky=True)
        b_var = sc.add(V, lambda e: e.tensor_tensor(
            out=TRE[:, 0:16], in0=TRE[:, 0:16], in1=TRE[:, 16:32], op=ALU.subtract),
            tag="b_var", risky=True)
        sc.add(A, lambda e: e.activation(
            out=TRE[:, 16:32], in_=TRE[:, 0:16], func=AF.Ln, bias=np.float32(EPS_LN),
            scale=1.0), deps=[b_var], tag="b_ln1")
        a_r1 = sc.add(A, lambda e: e.activation(
            out=RSTD1[:], in_=TRE[:, 16:32], func=AF.Exp, scale=-0.5), tag="b_r1")
        sc.add(V, lambda e: e.tensor_scalar(
            out=MU2[:], in0=S1T[:], scalar1=I1K, scalar2=None, op0=ALU.mult),
            deps=[a_r1], tag="b_mu2a", risky=True)
        sc.add(V, lambda e: e.tensor_tensor(
            out=MU2[:], in0=MU2[:], in1=MU1[:], op=ALU.subtract), tag="b_mu2b", risky=True)
        sc.add(V, lambda e: e.tensor_tensor(
            out=MU2[:], in0=MU2[:], in1=RSTD1[:], op=ALU.mult), tag="b_mu2c", risky=True)
        sc.add(V, lambda e: e.tensor_tensor(
            out=TRE[:, 16:32], in0=RSTD1[:], in1=RSTD1[:], op=ALU.mult),
            tag="b_rsq", risky=True)
        b_v2 = sc.add(V, lambda e: e.tensor_tensor(
            out=TRE[:, 16:32], in0=TRE[:, 16:32], in1=TRE[:, 0:16], op=ALU.mult),
            tag="b_v2", risky=True)
        sc.add(A, lambda e: e.activation(
            out=TRE[:, 32:48], in_=TRE[:, 16:32], func=AF.Ln, bias=np.float32(EPS_LN),
            scale=1.0), deps=[b_v2], tag="b_ln2")
        a_r2 = sc.add(A, lambda e: e.activation(
            out=RSTD2[:], in_=TRE[:, 32:48], func=AF.Exp, scale=-0.5), tag="b_r2")
        b_rt = sc.add(V, lambda e: e.tensor_tensor(
            out=RT[:], in0=RSTD1[:], in1=RSTD2[:], op=ALU.mult),
            deps=[a_r2], tag="b_rt", risky=True)
        b_nmr1 = sc.add(V, lambda e: e.scalar_tensor_tensor(
            out=NMR[:, 0:NT], in0=MU1[:], scalar=-1.0, in1=RSTD1[:],
            op0=ALU.mult, op1=ALU.mult), tag="b_nmr1", risky=True)
        b_nmr2 = sc.add(V, lambda e: e.scalar_tensor_tensor(
            out=NMR[:, NT:2 * NT], in0=MU2[:], scalar=-1.0, in1=RSTD2[:],
            op0=ALU.mult, op1=ALU.mult), tag="b_nmr2", risky=True)

        p2st = {"rmm_prev": None, "lgt_prev": None,
                "last_tmp_reader": [p1_user[2], None],
                "lte_ops": [], "last_ztr": [p1_user[3], last_red],
                "p1_zb_last": None,
                "xwin_dma": None, "last_zd": [None, None], "zd_ops": [],
                "rmm2": None}

        def emit_p2_tile(g):
            par = g % 2
            xl = sc.add(SY, lambda e, g=g, par=par: e.dma_start(
                out=XT[par][:], in_=x_d.ap()[g * 128:(g + 1) * 128, :]),
                deps=[last_xt_user[par]], dma=True, tag=f"xlb{g}")
            xt = XT[par]
            wpar = 0 if g == 15 else g % 2
            WB = TMP if wpar == 0 else XN
            z1 = sc.add(V, lambda e, xt=xt, g=g, WB=WB: e.tensor_scalar(
                out=WB[:], in0=xt[:], scalar1=MU1[:, g:g + 1],
                scalar2=RT[:, g:g + 1], op0=ALU.subtract, op1=ALU.mult),
                deps=[xl, b_rt, p2st["last_tmp_reader"][wpar]] +
                     ([p2st["xwin_dma"]] if wpar == 1 else []), tag=f"z1_{g}")
            last_xt_user[par] = z1
            if g == 15:
                t_xn[15] = sc.add(V, lambda e, xt=xt: e.tensor_scalar(
                    out=XN[:], in0=xt[:], scalar1=MU1[:, 15:16],
                    scalar2=RSTD1[:, 15:16], op0=ALU.subtract, op1=ALU.mult),
                    tag="xn15")
                last_xt_user[par] = t_xn[15]
                p2st["xwin_dma"] = sc.add(SY, lambda e: e.dma_start(
                    out=XWIN[:], in_=XN[96:128, :]), deps=[t_xn[15]], dma=True,
                    tag="xwin_dma")
            zpar = g % 2
            ZB = ZF if zpar == 0 else SQX
            tsl = ["B6", "B7"] if zpar == 0 else ["H01", "H23"]
            tps = PS_B if zpar == 0 else PS_H
            ZTFl = ZTF if zpar == 0 else APAD
            zg = sc.add(V, lambda e, ZB=ZB, WB=WB: e.tensor_tensor(
                out=ZB[:], in0=WB[:], in1=G3B[:], op=ALU.mult),
                deps=[z1, bcast_cp[0], p2st["last_ztr"][zpar],
                      p2st["last_zd"][zpar]],
                tag=f"zg_{g}")
            p2st["last_tmp_reader"][wpar] = zg
            zb = sc.add(V, lambda e, ZB=ZB: e.tensor_tensor(
                out=ZB[:], in0=ZB[:], in1=B3B[:], op=ALU.add),
                deps=[bcast_cp[1]], tag=f"zb_{g}")
            p2st["p1_zb_last"] = zb
            tr = None
            wdeps = ps_w(tsl, [zb, ld_idn])
            for c in range(8):
                tr = sc.add(T, lambda e, c=c, ZB=ZB, tps=tps: e.transpose(
                    tps[:, c * 128:(c + 1) * 128], ZB[:, c * 128:(c + 1) * 128],
                    IDN[:]), deps=wdeps if c == 0 else [], tag=f"ztr_{g}_{c}")
            ps_w_done(tsl, tr)
            p2st["last_ztr"][zpar] = tr
            zdw = sc.add(SY, lambda e, g=g, ZB=ZB: e.dma_start(
                out=zd_d.ap()[g * 128:(g + 1) * 128, :], in_=ZB[:]),
                deps=[zb], dma=True, tag=f"zd_{g}")
            p2st["last_zd"][zpar] = zdw
            p2st["zd_ops"].append(zdw)
            zcf = sc.add(A, lambda e, ZTFl=ZTFl, tps=tps: e.copy(
                ZTFl[:], tps[:, 0:1024]),
                deps=ps_r(tsl, [p2st["rmm2" if zpar else "rmm_prev"], tr]),
                tag=f"zcf_{g}")
            ps_r_done(tsl, zcf)

            def rmm(e, ZTFl=ZTFl):
                for kc in range(8):
                    e.matmul(PS_A[0:8, 0:128], RW[:, kc * 8:(kc + 1) * 8],
                             ZTFl[:, kc * 128:(kc + 1) * 128],
                             start=(kc == 0), stop=False)
                return e.matmul(PS_A[0:8, 0:128], RBR[:], ONESR[:],
                                start=False, stop=True)
            rmm_o = sc.add(T, rmm, deps=ps_w(["A4"], [zcf, ld_rw, ld_rbr, mz_ones]),
                           tag=f"rmm_{g}")
            ps_w_done(["A4"], rmm_o)
            p2st["rmm2" if zpar else "rmm_prev"] = rmm_o
            lgt_cp = sc.add(A, lambda e: e.copy(LGT[:], PS_A[0:8, 0:128]),
                            deps=ps_r(["A4"], [rmm_o, p2st["lgt_prev"]]), tag=f"lgt_{g}")
            ps_r_done(["A4"], lgt_cp)
            trl = sc.add(T, lambda e: e.transpose(
                PS_A[:, 512:520], LGT[:], IDN[0:8, 0:8]),
                deps=ps_w(["A5"], [lgt_cp]), tag=f"trl_{g}")
            ps_w_done(["A5"], trl)
            p2st["lgt_prev"] = trl
            lte = sc.add(A, lambda e, g=g: e.copy(
                bass.AP(LTE, g, [[NT * E, 128], [NT, E]]), PS_A[:, 512:520]),
                deps=ps_r(["A5"], [trl]), tag=f"lte_{g}")
            ps_r_done(["A5"], lte)
            p2st["lte_ops"].append(lte)

        emit_p2_tile(15)
        xwin_dma = p2st["xwin_dma"]
        ld_mask = sc.add(SY, lambda e: e.dma_start(out=MASK[:], in_=mask_d.ap()),
                         dma=True, tag="ld_mask")
        with ExitStack() as actx:
            XWT = sb("xwt", [128, 8 * WIN], BF16, c=actx)
            TF = sb("tf", [WIN, DM], c=actx)
            SQA = sb("sqa", [WIN, DM], c=actx)
            NORMS = sb("norms", [WIN, 3 * H], c=actx)
            INVN = sb("invn", [WIN, 3 * H], c=actx)

            xwt_tr = None
            wdeps = ps_w(["A4"], [xwin_dma, ld_idn])
            for c in range(8):
                xwt_tr = sc.add(T, lambda e, c=c: e.transpose(
                    PS_A[:, c * 32:(c + 1) * 32],
                    XWIN[:, c * 128:(c + 1) * 128], IDN[0:32, 0:32]),
                    deps=wdeps if c == 0 else [], tag=f"xwtr_{c}")
            ps_w_done(["A4"], xwt_tr)
            xwt_cp = sc.add(A, lambda e: e.copy(XWT[:], PS_A[:, 0:256]),
                            deps=ps_r(["A4"], [xwt_tr]), tag="xwc")
            ps_r_done(["A4"], xwt_cp)
            a_xwt = [xwt_cp] * 8

            def proj_tok(wslot, wdep, psl_names, psl_ap, tag):
                def emit(e):
                    for n in range(2):
                        for kc in range(8):
                            last = e.matmul(psl_ap[0:32, n * 512:(n + 1) * 512],
                                     XWT[:, kc * 32:(kc + 1) * 32],
                                     wslot[:, kc * 1024 + n * 512: kc * 1024 + (n + 1) * 512],
                                     start=(kc == 0), stop=(kc == 7))
                    return last
                o = sc.add(T, emit, deps=ps_w(psl_names, [wdep, a_xwt[7]]), tag=tag)
                ps_w_done(psl_names, o)
                return o

            tr_par = [0]

            def small_tr(in_ap, out_cols, ident_ap, deps, tag):
                sl = "A4" if tr_par[0] == 0 else "A5"
                pv = PS_A[:, 0:out_cols] if tr_par[0] == 0 else PS_A[:, 512:512 + out_cols]
                tr_par[0] ^= 1
                o = sc.add(T, lambda e, pv=pv, in_ap=in_ap, ident_ap=ident_ap:
                           e.transpose(pv, in_ap, ident_ap),
                           deps=ps_w([sl], deps), tag=tag)
                ps_w_done([sl], o)
                return pv, o, sl

            def l2norm_and_T(dst, ev_dep, tag, stage, nc0):
                rs_last = None
                for h in range(H):
                    rs_last = sc.add(A, lambda e, h=h, stage=stage, nc0=nc0:
                        e.activation(
                            out=XWIN[:, h * DH:(h + 1) * DH],
                            in_=stage[:, h * DH:(h + 1) * DH], func=AF.Square,
                            accum_out=NORMS[:, nc0 + h:nc0 + h + 1]),
                        deps=[ev_dep] if h == 0 else [],
                        tag=f"{tag}_rs{h}", risky=(h == H - 1))
                sc.add(A, lambda e, nc0=nc0: e.activation(
                    out=INVN[:, nc0:nc0 + H], in_=NORMS[:, nc0:nc0 + H],
                    func=AF.Ln), tag=f"{tag}_ln")
                aex = sc.add(A, lambda e, nc0=nc0: e.activation(
                    out=NORMS[:, nc0:nc0 + H], in_=INVN[:, nc0:nc0 + H],
                    func=AF.Exp, scale=0.5), tag=f"{tag}_ex")
                sc.add(V, lambda e, nc0=nc0: e.tensor_scalar(
                    out=INVN[:, nc0:nc0 + H], in0=NORMS[:, nc0:nc0 + H],
                    scalar1=EPS_NORM, scalar2=None,
                    op0=ALU.add), deps=[aex], tag=f"{tag}_eps", risky=True)
                sc.add(V, lambda e, nc0=nc0: e.reciprocal(
                    out=NORMS[:, nc0:nc0 + H], in_=INVN[:, nc0:nc0 + H]),
                    tag=f"{tag}_rcp", risky=True)
                scale_last = sc.add(V, lambda e, stage=stage, nc0=nc0: e.tensor_tensor(
                    out=stage[:].rearrange("p (h d) -> p h d", d=DH),
                    in0=stage[:].rearrange("p (h d) -> p h d", d=DH),
                    in1=bass.AP(NORMS, nc0, [[3 * H, WIN], [1, H], [0, DH]]),
                    op=ALU.mult), tag=f"{tag}_scale")
                last_tr = None
                wdeps = ps_w(["A4"], [scale_last])
                for c in range(8):
                    last_tr = sc.add(T, lambda e, c=c, stage=stage: e.transpose(
                        PS_A[:, c * 32:(c + 1) * 32],
                        stage[:, c * 128:(c + 1) * 128], IDN[0:32, 0:32]),
                        deps=wdeps if c == 0 else [], tag=f"{tag}_tr{c}")
                ps_w_done(["A4"], last_tr)
                ev = sc.add(A, lambda e: e.copy(dst[:], PS_A[:, 0:256]),
                            deps=ps_r(["A4"], [last_tr]), tag=f"{tag}_ev")
                ps_r_done(["A4"], ev)
                evs = [ev] * 8
                return last_tr, evs

            qmm = proj_tok(S0, ld_wq, ["H01"], PS_H[:, 0:1024], "qmm")
            q_ev = sc.add(A, lambda e: e.copy(
                TF[:], PS_H[0:32, 0:1024]),
                deps=ps_r(["H01"], [qmm]), tag="q_ev")
            ps_r_done(["H01"], q_ev)
            q_trl, q_evs = l2norm_and_T(QT, q_ev, "qn", TF, 0)

            kmm = proj_tok(S1, ld_wk, ["H23"], PS_H[:, 1024:2048], "kmm")
            k_ev = sc.add(A, lambda e: e.copy(
                SQA[:], PS_H[0:32, 1024:2048]),
                deps=ps_r(["H23"], [kmm]), tag="k_ev")
            ps_r_done(["H23"], k_ev)
            ld_wkp = load_w_1024(S0, wkp_d, "ld_wkp", [qmm])
            k_trl, k_evs = l2norm_and_T(KT, k_ev, "kn", SQA, H)

            ld_wv = load_w_1024(S1, wv_d, "ld_wv", [kmm])
            emit_pp(2, 18, [k_ev])
            kpmm = proj_tok(S0, ld_wkp, ["B6", "B7"], PS_B[:, 0:1024], "kpmm")
            kp_ev = sc.add(A, lambda e: e.copy(
                TF[:], PS_B[0:32, 0:1024]),
                deps=ps_r(["B6", "B7"], [kpmm, q_trl]), tag="kp_ev")
            ps_r_done(["B6", "B7"], kp_ev)
            kp_trl, kp_evs = l2norm_and_T(KPT, kp_ev, "kpn", TF, 2 * H)

            def vmm_emit(e):
                for fc in range(8):
                    for kc in range(8):
                        last = e.matmul(PS_B[:, fc * 32:(fc + 1) * 32],
                                 S1[:, kc * 1024 + fc * 128: kc * 1024 + fc * 128 + 128],
                                 XWT[:, kc * 32:(kc + 1) * 32],
                                 start=(kc == 0), stop=(kc == 7))
                return last
            vmm = sc.add(T, vmm_emit, deps=ps_w(["B6"], [ld_wv, a_xwt[7]]), tag="vmm")
            ps_w_done(["B6"], vmm)
            v_ev = sc.add(A, lambda e: e.copy(VT[:], PS_B[:, 0:256]),
                          deps=ps_r(["B6"], [vmm]), tag="v_ev")
            ps_r_done(["B6"], v_ev)

        ld_wo = load_w_1024(S1, wo_d, "ld_wo", [vmm])
        emit_pp(18, 32, [v_ev])

        for g in range(15):
            emit_p2_tile(g)
        mz_apad2 = sc.add(V, lambda e: e.memset(APAD[:], 0.0),
                          deps=[p2st["rmm2"]], tag="apad0b")

        smst = {}

        def emit_softmax():
            def nmax_exp(off):
                return bass.AP(TRE, off, [[128, 128], [0, E], [1, NT]])
            t1 = sc.add(V, lambda e: e.tensor_tensor(
                out=TRE[:, 0:64], in0=LTE[:, 0:64], in1=LTE[:, 64:128], op=ALU.max),
                deps=p2st["lte_ops"], tag="sm_t1", risky=True)
            sc.add(V, lambda e: e.tensor_tensor(
                out=TRE[:, 64:96], in0=TRE[:, 0:32], in1=TRE[:, 32:64], op=ALU.max),
                tag="sm_t2", risky=True)
            sc.add(V, lambda e: e.tensor_tensor(
                out=TRE[:, 96:112], in0=TRE[:, 64:80], in1=TRE[:, 80:96], op=ALU.max),
                tag="sm_t3", risky=True)
            sm_sub = sc.add(V, lambda e: e.tensor_tensor(
                out=EXB[:], in0=LTE[:], in1=nmax_exp(96), op=ALU.subtract),
                tag="sm_sub", risky=True)
            sm_exp = sc.add(A, lambda e: e.activation(
                out=AUX[:], in_=EXB[:], func=AF.Exp), deps=[sm_sub], tag="sm_exp")
            sc.add(V, lambda e: e.tensor_scalar(
                out=LTE[:], in0=EXB[:], scalar1=0.0, scalar2=None, op0=ALU.is_ge),
                deps=[sm_exp], tag="sm_msk", risky=True)
            sc.add(V, lambda e: e.tensor_scalar(
                out=LTE[:], in0=LTE[:], scalar1=-1.0, scalar2=1.0,
                op0=ALU.mult, op1=ALU.add), tag="sm_inv", risky=True)
            sc.add(V, lambda e: e.tensor_tensor(
                out=EXB[:], in0=AUX[:], in1=LTE[:], op=ALU.mult), tag="sm_p2",
                risky=True)
            sc.add(V, lambda e: e.tensor_tensor(
                out=TRE[:, 0:64], in0=EXB[:, 0:64], in1=EXB[:, 64:128], op=ALU.max),
                tag="sm_m1", risky=True)
            sc.add(V, lambda e: e.tensor_tensor(
                out=TRE[:, 64:96], in0=TRE[:, 0:32], in1=TRE[:, 32:64], op=ALU.max),
                tag="sm_m2", risky=True)
            sc.add(V, lambda e: e.tensor_tensor(
                out=TRE[:, 112:128], in0=TRE[:, 64:80], in1=TRE[:, 80:96], op=ALU.max),
                tag="sm_m3", risky=True)
            sc.add(V, lambda e: e.tensor_tensor(
                out=LTE[:], in0=AUX[:], in1=nmax_exp(112), op=ALU.is_ge),
                tag="sm_sel", risky=True)
            sc.add(V, lambda e: e.tensor_tensor(
                out=EXB[:], in0=AUX[:], in1=LTE[:], op=ALU.mult), tag="sm_pw",
                risky=True)
            sc.add(V, lambda e: e.tensor_tensor(
                out=TRE[:, 0:64], in0=EXB[:, 0:64], in1=EXB[:, 64:128], op=ALU.add),
                tag="sm_s1", risky=True)
            sc.add(V, lambda e: e.tensor_tensor(
                out=TRE[:, 64:96], in0=TRE[:, 0:32], in1=TRE[:, 32:64], op=ALU.add),
                tag="sm_s2", risky=True)
            sc.add(V, lambda e: e.tensor_tensor(
                out=TRE[:, 96:112], in0=TRE[:, 64:80], in1=TRE[:, 80:96], op=ALU.add),
                tag="sm_s3", risky=True)
            sc.add(V, lambda e: e.tensor_scalar(
                out=TRE[:, 96:112], in0=TRE[:, 96:112], scalar1=1e-8, scalar2=None,
                op0=ALU.max), tag="sm_cl", risky=True)
            sc.add(V, lambda e: e.reciprocal(
                out=TRE[:, 112:128], in_=TRE[:, 96:112]), tag="sm_rcp", risky=True)
            smst["w_fin"] = sc.add(V, lambda e: e.tensor_tensor(
                out=bass.AP(WALL, 0, [[NT * E, 128], [1, E], [E, NT]]),
                in0=EXB[:], in1=nmax_exp(112), op=ALU.mult), tag="sm_w", risky=True)
            for g in range(NT):
                t_w[g] = smst["w_fin"]


        if phase in ("p1", "tile1"):
            _finalize_partial()
            return nc


        zev = [None] * H
        atst = {"mt_readers": [], "ntj_readers": [], "prev_zmm": None}

        def emit_attn_head_a(c, hp):
            mt_readers = atst["mt_readers"]
            ntj_readers = atst["ntj_readers"]
            if hp == 0:
                mb_last = sc.add(V, lambda e, c=c: e.tensor_tensor(
                    out=MT[:].rearrange("p (j k) -> p j k", k=WIN),
                    in0=bass.AP(KT, WIN * c, [[8 * WIN, 128], [1, WIN], [0, WIN]]),
                    in1=bass.AP(KPT, WIN * c, [[8 * WIN, 128], [0, WIN], [1, WIN]]),
                    op=ALU.mult),
                    deps=[k_evs[c], kp_evs[c]] + mt_readers, tag=f"mt_{c}")
                atst["mt_readers"] = []
                nb_last = sc.add(V, lambda e, c=c: e.tensor_tensor(
                    out=ZF[:, 0:WIN * WIN].rearrange("p (j k) -> p j k", k=WIN),
                    in0=bass.AP(VT, WIN * c, [[8 * WIN, 128], [1, WIN], [0, WIN]]),
                    in1=bass.AP(VT, WIN * c, [[8 * WIN, 128], [0, WIN], [1, WIN]]),
                    op=ALU.mult),
                    deps=[v_ev, p2st["p1_zb_last"]] + ntj_readers, tag=f"nt_{c}")
                atst["ntj_readers"] = []
                atst["mb_last"] = mb_last
                atst["nb_last"] = nb_last
            h = 2 * c + hp

            def amm_emit(e, c=c, hp=hp):
                b0 = 64 * hp
                for n in range(2):
                    last = e.matmul(PS_B[0:32, n * 512:(n + 1) * 512],
                             QT[b0:b0 + 64, c * 32:(c + 1) * 32],
                             MT[b0:b0 + 64, n * 512:(n + 1) * 512],
                             start=True, stop=True)
                return last
            amm = sc.add(T, amm_emit,
                         deps=ps_w(["B6", "B7"], [q_evs[c], atst["mb_last"]]),
                         tag=f"amm_{h}")
            ps_w_done(["B6", "B7"], amm)
            atst["mt_readers"].append(amm)
            t_msk = sc.add(V, lambda e: e.scalar_tensor_tensor(
                out=PF[:], in0=PS_B[0:32, 0:1024], scalar=0.125, in1=MASK[:],
                op0=ALU.mult, op1=ALU.add),
                deps=ps_r(["B6", "B7"], [amm, ld_mask]), tag=f"msk_{h}")
            ps_r_done(["B6", "B7"], t_msk)
            amax = sc.add(V, lambda e: e.reduce_max(
                out=SCR[0:32, 0:1], in_=PF[:], axis=AX.X, negate=True),
                tag=f"amax_{h}", risky=True)
            a_pexh = sc.add(A, lambda e: e.activation(
                out=PSC[:], in_=PF[:], func=AF.Exp, bias=SCR[0:32, 0:1],
                scale=1.0, accum_out=PSR[:]),
                deps=[amax], tag=f"aexp_{h}")
            sc.add(V, lambda e: e.reciprocal(
                out=SCR[0:32, 1:2], in_=PSR[:]), deps=[a_pexh],
                tag=f"arcp_{h}", risky=True)
            atst["t_psc"] = sc.add(V, lambda e: e.tensor_scalar(
                out=PF[:], in0=PSC[:], scalar1=SCR[0:32, 1:2],
                scalar2=None, op0=ALU.mult), tag=f"apn_{h}")

        def emit_attn_head_b(c, hp):
            h = 2 * c + hp
            t_psc = atst["t_psc"]
            nb_last = atst["nb_last"]
            prev_zmm = atst["prev_zmm"]
            ptr_last = None
            wdeps = ps_w(["B6"], [t_psc])
            for b in range(8):
                ptr_last = sc.add(T, lambda e, b=b: e.transpose(
                    PS_B[:, b * 32:(b + 1) * 32],
                    PF[:, b * 128:(b + 1) * 128], IDN[0:32, 0:32]),
                    deps=wdeps if b == 0 else [], tag=f"ptr_{h}_{b}")
            ps_w_done(["B6"], ptr_last)
            pt_last_ev = sc.add(A, lambda e: e.copy(PT[:], PS_B[:, 0:256]),
                                deps=ps_r(["B6"], [ptr_last, prev_zmm]),
                                tag=f"ptc_{h}")
            ps_r_done(["B6"], pt_last_ev)
            ntr_last = None
            wdeps = ps_w(["B7"], [nb_last])
            for b in range(8):
                ntr_last = sc.add(T, lambda e, b=b, hp=hp: e.transpose(
                    PS_B[:, 512 + b * 64:512 + (b + 1) * 64],
                    ZF[64 * hp:64 * hp + 64, b * 128:(b + 1) * 128],
                    IDN[64 * hp:64 * hp + 64, 64 * hp:64 * hp + 64]),
                    deps=wdeps if b == 0 else [], tag=f"ntr_{h}_{b}")
            ps_w_done(["B7"], ntr_last)
            ntj_last_ev = sc.add(A, lambda e: e.copy(NTJ[:], PS_B[:, 512:1024]),
                                 deps=ps_r(["B7"], [ntr_last]),
                                 tag=f"ntc_{h}")
            ps_r_done(["B7"], ntj_last_ev)

            def zmm_emit(e):
                for b in range(8):
                    last = e.matmul(PS_B[0:64, 0:32], NTJ[:, b * 64:(b + 1) * 64],
                             PT[:, b * 32:(b + 1) * 32],
                             start=(b == 0), stop=(b == 7))
                return last
            zmm = sc.add(T, zmm_emit,
                         deps=ps_w(["B6"], [pt_last_ev, ntj_last_ev]),
                         tag=f"zmm_{h}")
            ps_w_done(["B6"], zmm)
            atst["ntj_readers"].append(zmm)
            atst["prev_zmm"] = zmm
            zev[h] = sc.add(A, lambda e, hp=hp, c=c: e.copy(
                ZTB[64 * hp:64 * hp + 64, c * 32:(c + 1) * 32], PS_B[0:64, 0:32]),
                deps=ps_r(["B6"], [zmm]), tag=f"zev_{h}")
            ps_r_done(["B6"], zev[h])

        aost = {}

        def emit_attn_out_mm():
            def aomm_emit(e):
                for n in range(2):
                    for kc in range(8):
                        last = e.matmul(PS_B[0:32, n * 512:(n + 1) * 512],
                                 ZTB[:, kc * 32:(kc + 1) * 32],
                                 S1[:, kc * 1024 + n * 512: kc * 1024 + (n + 1) * 512],
                                 start=(kc == 0), stop=(kc == 7))
                return last
            aomm = sc.add(T, aomm_emit,
                          deps=ps_w(["B6", "B7"], [ld_wo] + zev), tag="aomm")
            ps_w_done(["B6", "B7"], aomm)
            a_aev = sc.add(A, lambda e: e.copy(AOUT[:], PS_B[0:32, 0:1024]),
                           deps=ps_r(["B6", "B7"], [aomm]), tag="a_aev")
            ps_r_done(["B6", "B7"], a_aev)
            aost["a_aev"] = a_aev
            apad_dma = sc.add(SY, lambda e: e.dma_start(
                out=APAD[96:128, :], in_=AOUT[:]),
                deps=[a_aev, mz_apad2], dma=True, tag="apad_dma")
            aost["apad_dma"] = apad_dma

        def emit_attn_out_f15():
            apad_dma = aost["apad_dma"]
            x15 = sc.add(SY, lambda e: e.dma_start(
                out=XT[1][:], in_=x_d.ap()[15 * 128:16 * 128, :]),
                deps=[last_xt_user[1]], dma=True, tag="x15")
            f_xn = sc.add(V, lambda e: e.tensor_scalar(
                out=XN[:], in0=XT[1][:], scalar1=MU1[:, 15:16],
                scalar2=RSTD1[:, 15:16], op0=ALU.subtract, op1=ALU.mult),
                deps=[x15, xwin_dma], tag="f15_xn")
            last_xt_user[1] = f_xn
            f_a = sc.add(V, lambda e: e.tensor_tensor(
                out=TMP[:], in0=XN[:], in1=APAD[:], op=ALU.add),
                deps=[apad_dma], tag="f15_a")
            sc.add(V, lambda e: e.tensor_tensor(
                out=SQX[:], in0=TMP[:], in1=TMP[:], op=ALU.mult),
                deps=[p2st["last_ztr"][1], p2st["last_zd"][1]], tag="f15_sq")
            sc.add(V, lambda e: e.reduce_sum(
                out=SCR[:, 0:1], in_=TMP[:], axis=AX.X), tag="f15_s", risky=True)
            sc.add(V, lambda e: e.tensor_scalar(
                out=MU2[:, 15:16], in0=SCR[:, 0:1], scalar1=I1K, scalar2=None,
                op0=ALU.mult), tag="f15_mu", risky=True)
            sc.add(V, lambda e: e.tensor_tensor(
                out=SCR[:, 1:2], in0=MU2[:, 15:16], in1=MU2[:, 15:16],
                op=ALU.mult), tag="f15_musq")
            sc.add(V, lambda e: e.reduce_sum(
                out=SCR[:, 2:3], in_=SQX[:], axis=AX.X), tag="f15_ss", risky=True)
            sc.add(V, lambda e: e.tensor_scalar(
                out=SCR[:, 2:3], in0=SCR[:, 2:3], scalar1=I1K, scalar2=None,
                op0=ALU.mult), tag="f15_va", risky=True)
            f_v = sc.add(V, lambda e: e.tensor_tensor(
                out=SCR[:, 2:3], in0=SCR[:, 2:3], in1=SCR[:, 1:2],
                op=ALU.subtract), tag="f15_vb")
            sc.add(A, lambda e: e.activation(
                out=SCR[:, 3:4], in_=SCR[:, 2:3], func=AF.Ln, bias=np.float32(EPS_LN),
                scale=1.0), deps=[f_v], tag="f15_ln")
            f_rstd = sc.add(A, lambda e: e.activation(
                out=RSTD2[:, 15:16], in_=SCR[:, 3:4], func=AF.Exp, scale=-0.5),
                tag="f15_rstd")
            nmr2f = sc.add(V, lambda e: e.scalar_tensor_tensor(
                out=NMR[:, NT + 15:NT + 16], in0=MU2[:, 15:16],
                scalar=-1.0, in1=RSTD2[:, 15:16],
                op0=ALU.mult, op1=ALU.mult), deps=[f_rstd], tag="f15_nmr2",
                risky=True)
            aost["apad_dma"] = apad_dma
            aost["f_rstd"] = f_rstd
            aost["nmr2f"] = nmr2f
            if dbg:
                sc.add(SY, lambda e: e.dma_start(out=dbg_apad.ap(), in_=APAD[:]),
                       deps=[apad_dma, f_a], dma=True, tag="dbg_apad")
                sc.add(SY, lambda e: e.dma_start(out=dbg_ztb.ap(), in_=ZTB[:]),
                       deps=zev, dma=True, tag="dbg_ztb")


        attx.close()
        W1S = sb("w1s", [128, 16 * 1024], BF16)
        W2S = sb("w2s", [128, 16 * 1024], BF16)
        IDNB = sb("idnb", [128, 128], BF16)
        LTRIS = sb("ltris", [128, 128])
        TIDTS = sb("tidts", [128, NT])
        EMAJS = sb("emajs", [128, 128])
        DEFS = sb("defs", [128, 4])
        SIOS = sb("sios", [1, 64])
        ONESC = sb("onesc", [128, 1])
        MSKE = sb("mske", [128, 128])
        RNKE = sb("rnke", [128, 128])
        M0E = sb("m0e", [128, 128])
        M1E = sb("m1e", [128, 128])
        PFXE = sb("pfxe", [128, 128])
        OFFBE = sb("offbe", [128, 128])
        POSE = sb("pose", [128, 128])
        RS1 = sb("rs1", [128, 128])
        FLD = sb("fld", [128, 96])
        ACCS = sb("accs", [128, 128])
        RW0 = sb("rowscr", [1, 1536])
        VREC = sb("vrec", [128, 2 * 64])
        IRECU = sb("irecu", [128, 2 * NT], U32)
        MASKE2 = sb("maske2", [128, 8 * NSLOT])
        EBF = sb("ebf", [128, 128])
        WIDXF = sb("widxf", [128, 128])
        WIDXU = sb("widxu", [128, 128], U32)
        SMET = sb("smet", [128, 16])
        SIDU = sb("sidu", [128, 8], U32)
        B1SL = sb("b1sl", [128, 32])

        st_ops = [None] * NT
        wtr_cp = [None] * NT

        def emit_wtr_chunk(gs):
            for g in gs:
                sl = "B6" if g % 2 == 0 else "B7"
                off = 0 if g % 2 == 0 else 512
                tr = sc.add(T, lambda e, g=g, off=off: e.transpose(
                    PS_B[0:8, off:off + 128], WALL[:, g * 8:(g + 1) * 8], IDN[:]),
                    deps=ps_w([sl], [t_w[g], ld_idn]), tag=f"wtr_{g}")
                ps_w_done([sl], tr)
                cp = sc.add(A, lambda e, g=g, off=off: e.copy(
                    WTRALL[0:8, g * 128:(g + 1) * 128], PS_B[0:8, off:off + 128]),
                    deps=ps_r([sl], [tr]), tag=f"wtrcp_{g}")
                ps_r_done([sl], cp)
                wtr_cp[g] = cp

        emit_softmax()
        for c_ in range(8):
            for hp_ in range(2):
                emit_attn_head_a(c_, hp_)
                emit_attn_head_b(c_, hp_)
        emit_attn_out_mm()
        emit_attn_out_f15()
        emit_wtr_chunk(range(NT))
        if dbg:
            sc.add(SY, lambda e: e.dma_start(out=dbg_wall.ap(), in_=WALL[:]),
                   deps=[smst["w_fin"]], dma=True, tag="dbg_wall")
            sc.add(SY, lambda e: e.dma_start(
                out=dbg_stats.ap()[:, 0:NT], in_=MU1[:]), deps=[smst["w_fin"]],
                dma=True, tag="dbg_mu1")
            sc.add(SY, lambda e: e.dma_start(
                out=dbg_stats.ap()[:, NT:2 * NT], in_=RSTD1[:]), dma=True, tag="dbg_r1")
            sc.add(SY, lambda e: e.dma_start(
                out=dbg_stats.ap()[:, 2 * NT:3 * NT], in_=MU2[:]), dma=True, tag="dbg_mu2")
            sc.add(SY, lambda e: e.dma_start(
                out=dbg_stats.ap()[:, 3 * NT:4 * NT], in_=RSTD2[:]), dma=True, tag="dbg_r2")

        ld_ltri = sc.add(SY, lambda e: e.dma_start(out=LTRIS[:], in_=ltri_d.ap()),
                         dma=True, tag="ld_ltri")
        ld_tidt = sc.add(SY, lambda e: e.dma_start(out=TIDTS[:], in_=tidt_d.ap()),
                         dma=True, tag="ld_tidt")
        ld_emaj = sc.add(SY, lambda e: e.dma_start(out=EMAJS[:], in_=emaj_d.ap()),
                         dma=True, tag="ld_emaj")
        ld_defs = sc.add(SY, lambda e: e.dma_start(out=DEFS[:], in_=def4_d.ap()),
                         dma=True, tag="ld_defs")
        ld_sio = sc.add(SY, lambda e: e.dma_start(out=SIOS[:], in_=sio_d.ap()),
                        dma=True, tag="ld_sio")
        mz_onesc = sc.add(V, lambda e: e.memset(ONESC[:], 1.0), tag="onesc")
        idnb_cp = sc.add(V, lambda e: e.tensor_copy(out=IDNB[:], in_=IDN[:]),
                         deps=[ld_idn], tag="idnb")
        def_dmas = []
        for s_ in range(NSLOT):
            def_dmas.append(sc.add(SY, lambda e, s_=s_: e.dma_start(
                out=slot_d.ap()[s_ * 128:(s_ + 1) * 128, :], in_=DEFS[:]),
                deps=[ld_defs], dma=True, tag=f"sdef{s_}"))

        breg_holder = {}

        def get_breg(e):
            if "r" not in breg_holder:
                rg = e.register("bnd")
                r = rg.__enter__()
                e.reg_mov(r, 1023)
                breg_holder["r"] = r
            return breg_holder["r"]

        def gp_bridge(deps, tag):
            return sc.add(GP, lambda e: e.tensor_copy(
                out=SCR[0:1, 7:8], in_=ONESR[0:1, 0:1]), deps=deps, tag=tag)

        wfin = smst["w_fin"]
        walle = bass.AP(WALL, 0, [[NT * E, 128], [1, E], [E, NT]])
        r_msk = sc.add(V, lambda e: e.tensor_scalar(
            out=MSKE[:].rearrange("p (e g) -> p e g", g=NT), in0=walle,
            scalar1=0.0, scalar2=None, op0=ALU.is_gt),
            deps=[wfin], tag="r_msk", risky=True)
        sc.add(V, lambda e: e.tensor_copy(out=RS1[:, 0:16], in_=MSKE[:, 0:16]),
               tag="r_p1a", risky=True)
        sc.add(V, lambda e: e.tensor_tensor(
            out=RS1[:, 16:128], in0=MSKE[:, 16:128], in1=MSKE[:, 0:112],
            op=ALU.add), tag="r_p1b", risky=True)
        sc.add(V, lambda e: e.tensor_copy(out=RNKE[:, 0:32], in_=RS1[:, 0:32]),
               tag="r_p2a", risky=True)
        sc.add(V, lambda e: e.tensor_tensor(
            out=RNKE[:, 32:128], in0=RS1[:, 32:128], in1=RS1[:, 0:96],
            op=ALU.add), tag="r_p2b", risky=True)
        sc.add(V, lambda e: e.tensor_copy(out=RS1[:, 0:64], in_=RNKE[:, 0:64]),
               tag="r_p3a", risky=True)
        sc.add(V, lambda e: e.tensor_tensor(
            out=RS1[:, 64:128], in0=RNKE[:, 64:128], in1=RNKE[:, 0:64],
            op=ALU.add), tag="r_p3b", risky=True)
        sc.add(V, lambda e: e.tensor_tensor(
            out=RNKE[:], in0=RS1[:], in1=MSKE[:], op=ALU.subtract),
            tag="r_exc", risky=True)
        sc.add(V, lambda e: e.tensor_scalar(
            out=M0E[:], in0=RNKE[:], scalar1=0.0, scalar2=None, op0=ALU.is_equal),
            tag="r_m0a", risky=True)
        sc.add(V, lambda e: e.tensor_tensor(
            out=M0E[:], in0=M0E[:], in1=MSKE[:], op=ALU.mult), tag="r_m0b",
            risky=True)
        sc.add(V, lambda e: e.tensor_scalar(
            out=M1E[:], in0=RNKE[:], scalar1=1.0, scalar2=None, op0=ALU.is_equal),
            tag="r_m1a", risky=True)
        r_m1 = sc.add(V, lambda e: e.tensor_tensor(
            out=M1E[:], in0=M1E[:], in1=MSKE[:], op=ALU.mult), tag="r_m1b",
            risky=True)
        pfx_mm = None
        wdeps0 = ps_w(["B6"], [r_msk, ld_ltri])
        for e_ in range(E):
            pfx_mm = sc.add(T, lambda e, e_=e_: e.matmul(
                PS_B[:, e_ * 16:(e_ + 1) * 16], LTRIS[:],
                MSKE[:, e_ * 16:(e_ + 1) * 16], start=True, stop=True),
                deps=wdeps0 if e_ == 0 else [], tag=f"r_pfx{e_}")
        ps_w_done(["B6"], pfx_mm)
        cnt_mm = sc.add(T, lambda e: e.matmul(
            PS_B[0:1, 512:640], ONESC[:], MSKE[:], start=True, stop=True),
            deps=ps_w(["B7"], [mz_onesc, r_msk]), tag="r_cnt")
        ps_w_done(["B7"], cnt_mm)
        pfx_cp = sc.add(A, lambda e: e.copy(PFXE[:], PS_B[:, 0:128]),
                        deps=ps_r(["B6"], [pfx_mm]), tag="r_pfxcp")
        ps_r_done(["B6"], pfx_cp)
        crow_cp = sc.add(A, lambda e: e.copy(RW0[0:1, 0:128], PS_B[0:1, 512:640]),
                         deps=ps_r(["B7"], [cnt_mm]), tag="r_crowcp")
        ps_r_done(["B7"], crow_cp)
        X0, X1, X2, X3, X4 = 0, 128, 256, 384, 512

        def blk(base, off, n):
            return bass.AP(RW0, base + off, [[1536, 1], [16, E], [1, n]])
        rp = sc.add(V, lambda e: e.tensor_copy(out=blk(X1, 0, 1), in_=blk(X0, 0, 1)),
                    deps=[crow_cp], tag="rg1a", risky=True)
        sc.add(V, lambda e: e.tensor_tensor(
            out=blk(X1, 1, 15), in0=blk(X0, 1, 15), in1=blk(X0, 0, 15),
            op=ALU.add), tag="rg1b", risky=True)
        sc.add(V, lambda e: e.tensor_copy(out=blk(X2, 0, 2), in_=blk(X1, 0, 2)),
               tag="rg2a", risky=True)
        sc.add(V, lambda e: e.tensor_tensor(
            out=blk(X2, 2, 14), in0=blk(X1, 2, 14), in1=blk(X1, 0, 14),
            op=ALU.add), tag="rg2b", risky=True)
        sc.add(V, lambda e: e.tensor_copy(out=blk(X3, 0, 4), in_=blk(X2, 0, 4)),
               tag="rg3a", risky=True)
        sc.add(V, lambda e: e.tensor_tensor(
            out=blk(X3, 4, 12), in0=blk(X2, 4, 12), in1=blk(X2, 0, 12),
            op=ALU.add), tag="rg3b", risky=True)
        sc.add(V, lambda e: e.tensor_copy(out=blk(X4, 0, 8), in_=blk(X3, 0, 8)),
               tag="rg4a", risky=True)
        sc.add(V, lambda e: e.tensor_tensor(
            out=blk(X4, 8, 8), in0=blk(X3, 8, 8), in1=blk(X3, 0, 8),
            op=ALU.add), tag="rg4b", risky=True)
        sc.add(V, lambda e: e.memset(blk(X1, 0, 1), 0.0), tag="rexc0", risky=True)
        rexc1_o = sc.add(V, lambda e: e.tensor_copy(
            out=blk(X1, 1, 15), in_=blk(X4, 0, 15)), tag="rexc1", risky=True)
        CNT = 640
        rcnt_o = sc.add(V, lambda e: e.tensor_copy(
            out=bass.AP(RW0, CNT, [[1536, 1], [1, E]]),
            in_=bass.AP(RW0, X4 + 15, [[1536, 1], [16, E]])), tag="rcnt",
            risky=True)
        rpcs_list = []
        for e_ in range(E):
            rpm = sc.add(V, lambda e, e_=e_: e.tensor_scalar(
                out=XN[0:1, e_ * 16:e_ * 16 + 16], in0=SIOS[0:1, 0:16],
                scalar1=RW0[0:1, 640 + e_:641 + e_], scalar2=None,
                op0=ALU.is_lt), deps=[ld_sio, rcnt_o], tag=f"rpcm{e_}",
                risky=True, chain=False)
            rpcs_list.append(sc.add(V, lambda e, e_=e_: e.reduce_sum(
                out=RW0[0:1, 672 + e_:673 + e_], in_=XN[0:1, e_ * 16:e_ * 16 + 16],
                axis=AX.X), deps=[rpm], tag=f"rpcs{e_}", risky=True,
                chain=False))
        rb_list = [sc.add(V, lambda e: e.memset(RW0[0:1, 680:681], 0.0),
                          tag="rb0", risky=True)]
        for e_ in range(E):
            rb_list.append(sc.add(V, lambda e, e_=e_: e.scalar_tensor_tensor(
                out=RW0[0:1, 681 + e_:682 + e_], in0=RW0[0:1, 672 + e_:673 + e_],
                scalar=128.0, in1=RW0[0:1, 680 + e_:681 + e_],
                op0=ALU.mult, op1=ALU.add), deps=[rpcs_list[e_]],
                tag=f"rb{e_ + 1}", risky=True))
        SLR, FTR, TMPR, MER = 704, 768, 384, 832
        for base, bo, opx in ((SLR, 681, ALU.is_ge), (FTR, 680, ALU.is_equal)):
            cmps = []
            for e_ in range(E):
                cmps.append(sc.add(V, lambda e, e_=e_, bo=bo, opx=opx:
                    e.tensor_scalar(
                        out=XN[0:1, 128 + e_ * 64:128 + e_ * 64 + 64],
                        in0=SIOS[:], scalar1=RW0[0:1, bo + e_:bo + e_ + 1],
                        scalar2=None, op0=opx),
                    deps=[ld_sio, rb_list[e_ + (1 if bo == 681 else 0)]],
                    tag=f"cm{base}_{e_}", risky=True, chain=False))
            lv1 = []
            for j in range(4):
                lv1.append(sc.add(V, lambda e, j=j: e.tensor_tensor(
                    out=XN[0:1, 640 + j * 64:640 + j * 64 + 64],
                    in0=XN[0:1, 128 + 2 * j * 64:128 + 2 * j * 64 + 64],
                    in1=XN[0:1, 128 + (2 * j + 1) * 64:128 + (2 * j + 1) * 64 + 64],
                    op=ALU.add), deps=[cmps[2 * j], cmps[2 * j + 1]],
                    tag=f"t1{base}_{j}", risky=True, chain=False))
            lv2 = []
            for j in range(2):
                lv2.append(sc.add(V, lambda e, j=j: e.tensor_tensor(
                    out=XN[0:1, 896 + j * 64:896 + j * 64 + 64],
                    in0=XN[0:1, 640 + 2 * j * 64:640 + 2 * j * 64 + 64],
                    in1=XN[0:1, 640 + (2 * j + 1) * 64:640 + (2 * j + 1) * 64 + 64],
                    op=ALU.add), deps=[lv1[2 * j], lv1[2 * j + 1]],
                    tag=f"t2{base}_{j}", risky=True, chain=False))
            sc.add(V, lambda e, base=base: e.tensor_tensor(
                out=RW0[0:1, base:base + 64], in0=XN[0:1, 896:960],
                in1=XN[0:1, 960:1024], op=ALU.add), deps=lv2,
                tag=f"t3{base}", risky=True)
        sc.add(V, lambda e: e.tensor_scalar(
            out=RW0[0:1, FTR:FTR + 64], in0=RW0[0:1, FTR:FTR + 64], scalar1=1.0,
            scalar2=None, op0=ALU.is_ge), tag="rftc", risky=True)
        for e_ in range(E):
            sc.add(V, lambda e, e_=e_: e.tensor_scalar(
                out=bass.AP(RW0, MER + e_ * NSLOT, [[1536, 1], [1, NSLOT]]),
                in0=RW0[0:1, SLR:SLR + NSLOT], scalar1=float(e_), scalar2=None,
                op0=ALU.is_equal), tag=f"rme{e_}", risky=True, chain=False)
        r_eb = sc.add(V, lambda e: e.tensor_scalar(
            out=RW0[0:1, SLR:SLR + 64], in0=RW0[0:1, SLR:SLR + 64], scalar1=128.0,
            scalar2=None, op0=ALU.mult), tag="r_eb", risky=True)
        r_obr = None
        for e_ in range(E):
            r_obr = sc.add(V, lambda e, e_=e_: e.tensor_scalar(
                out=RW0[0:1, X2 + e_ * 16:X2 + e_ * 16 + 16],
                in0=RW0[0:1, X1 + e_ * 16:X1 + e_ * 16 + 16],
                scalar1=RW0[0:1, 680 + e_:681 + e_], scalar2=None, op0=ALU.add),
                deps=[rexc1_o], tag=f"robr{e_}", risky=True)
        ob_mm = sc.add(T, lambda e: e.matmul(
            PS_H[0:128, 0:128], ONESR[:], RW0[0:1, X2:X2 + 128],
            start=True, stop=True),
            deps=ps_w(["H01"], [r_obr, mz_ones]), tag="r_obmm")
        ps_w_done(["H01"], ob_mm)
        ob_cp = sc.add(A, lambda e: e.copy(OFFBE[:], PS_H[:, 0:128]),
                       deps=ps_r(["H01"], [ob_mm]), tag="r_obcp")
        ps_r_done(["H01"], ob_cp)
        pk_mm = sc.add(T, lambda e: e.matmul(
            PS_H[0:128, 1024:1024 + 440], ONESR[:], RW0[0:1, SLR:SLR + 440],
            start=True, stop=True),
            deps=ps_w(["H23"], [r_eb]), tag="r_pkmm")
        ps_w_done(["H23"], pk_mm)
        pk_cp = sc.add(A, lambda e: e.copy(EBF[:, 0:128], PS_H[:, 1024:1152]),
                       deps=ps_r(["H23"], [pk_mm]), tag="r_pkcp")
        me_cp = sc.add(A, lambda e: e.copy(
            MASKE2[:], PS_H[:, 1152:1152 + 8 * NSLOT]),
            deps=[pk_cp], tag="r_mecp")
        ps_r_done(["H23"], me_cp)
        w_i1 = sc.add(V, lambda e: e.tensor_scalar(
            out=WIDXF[:, 64:128], in0=EBF[:, 0:64], scalar1=TIDTS[:, 0:1],
            scalar2=None, op0=ALU.add), deps=[pk_cp, ld_tidt], tag="w_i1",
            risky=True)
        w_o1 = sc.add(V, lambda e: e.tensor_scalar(
            out=FLD[:, 64:65], in0=TIDTS[:, 0:1], scalar1=1.0, scalar2=100000.0,
            op0=ALU.is_ge, op1=ALU.mult), deps=[ld_tidt], tag="w_o1", risky=True)
        w_i2 = sc.add(V, lambda e: e.tensor_scalar(
            out=FLD[:, 0:64], in0=EBF[:, 64:128], scalar1=-1.0,
            scalar2=1.0, op0=ALU.mult, op1=ALU.add), tag="w_i2", risky=True)
        w_i2b = sc.add(V, lambda e: e.tensor_scalar(
            out=FLD[:, 0:64], in0=FLD[:, 0:64], scalar1=FLD[:, 64:65],
            scalar2=None, op0=ALU.mult), tag="w_i2b", risky=True)
        w_cl = sc.add(V, lambda e: e.tensor_scalar(
            out=WIDXF[:, 64:128], in0=WIDXF[:, 64:128], scalar1=1023.0,
            scalar2=None, op0=ALU.min), tag="w_cl", risky=True)
        w_i3 = sc.add(V, lambda e: e.tensor_tensor(
            out=WIDXF[:, 0:64], in0=WIDXF[:, 64:128], in1=FLD[:, 0:64],
            op=ALU.add), tag="w_i3", risky=True)
        w_iu = sc.add(V, lambda e: e.tensor_copy(out=WIDXU[:], in_=WIDXF[:]),
                      tag="w_iu", risky=True)
        r_pose = sc.add(V, lambda e: e.tensor_tensor(
            out=POSE[:], in0=PFXE[:], in1=OFFBE[:], op=ALU.add),
            deps=[pfx_cp, ob_cp], tag="r_pose", risky=True)
        acc_fin = []
        uregs = (RS1, RNKE, MSKE, PFXE, OFFBE, EBF)
        ui = 0
        for k, MK in ((0, M0E), (1, M1E)):
            for qi, src_ in enumerate(("w", "pos", "e")):
                U = uregs[ui]
                ui += 1
                if src_ == "w":
                    op1 = sc.add(V, lambda e, MK=MK, U=U: e.tensor_tensor(
                        out=U[:].rearrange("p (e g) -> p e g", g=NT),
                        in0=MK[:].rearrange("p (e g) -> p e g", g=NT),
                        in1=walle, op=ALU.mult), deps=[r_m1],
                        tag=f"ac{k}{qi}a", risky=True, chain=False)
                elif src_ == "pos":
                    op1 = sc.add(V, lambda e, MK=MK, U=U: e.tensor_tensor(
                        out=U[:], in0=MK[:], in1=POSE[:], op=ALU.mult),
                        deps=[r_pose], tag=f"ac{k}{qi}a", risky=True,
                        chain=False)
                else:
                    op1 = sc.add(V, lambda e, MK=MK, U=U: e.tensor_tensor(
                        out=U[:], in0=MK[:], in1=EMAJS[:], op=ALU.mult),
                        deps=[ld_emaj], tag=f"ac{k}{qi}a", risky=True,
                        chain=False)
                op2 = sc.add(V, lambda e, U=U: e.tensor_tensor(
                    out=U[:, 0:64], in0=U[:, 0:64], in1=U[:, 64:128],
                    op=ALU.add), deps=[op1], tag=f"ac{k}{qi}b", risky=True,
                    chain=False)
                op3 = sc.add(V, lambda e, U=U: e.tensor_tensor(
                    out=U[:, 0:32], in0=U[:, 0:32], in1=U[:, 32:64],
                    op=ALU.add), deps=[op2], tag=f"ac{k}{qi}c", risky=True,
                    chain=False)
                dst = k * 64 + qi * 16
                acc_fin.append(sc.add(V, lambda e, dst=dst, U=U: e.tensor_tensor(
                    out=ACCS[:, dst:dst + 16], in0=U[:, 0:16],
                    in1=U[:, 16:32], op=ALU.add), deps=[op3],
                    tag=f"ac{k}{qi}d", risky=True, chain=False))
        r_sr1 = sc.add(V, lambda e: e.tensor_scalar(
            out=RS1[:, 0:16], in0=TIDTS[:], scalar1=2048.0, scalar2=None,
            op0=ALU.add), deps=[ld_tidt], tag="r_sr1", risky=True)
        rec_ops = []
        for k in range(2):
            for c, srcap in ((0, TIDTS[:]), (1, ACCS[:, k * 64:k * 64 + 16]),
                             (2, TIDTS[:] if k == 0 else RS1[:, 0:16]),
                             (3, ACCS[:, k * 64 + 32:k * 64 + 48])):
                rec_ops.append(sc.add(V, lambda e, k=k, c=c, srcap=srcap:
                    e.tensor_copy(
                        out=bass.AP(VREC, k * 64 + c, [[128, 128], [4, 16]]),
                        in_=srcap), deps=[r_sr1] + acc_fin,
                    tag=f"vr{k}{c}", risky=True, chain=False))
            rec_ops.append(sc.add(V, lambda e, k=k: e.tensor_copy(
                out=IRECU[:, k * 16:(k + 1) * 16],
                in_=ACCS[:, k * 64 + 16:k * 64 + 32]), deps=acc_fin,
                tag=f"ir{k}", risky=True, chain=False))
        scat0 = []
        for k in range(2):
            for g in range(NT):
                scat0.append(sc.add(GP, lambda e, k=k, g=g: e.indirect_dma_start(
                    out=slot_d.ap(), out_offset=IOA(
                        ap=IRECU[:, k * 16 + g:k * 16 + g + 1], axis=0),
                    in_=bass.AP(VREC, k * 64 + g * 4, [[128, 128], [1, 4]]),
                    in_offset=None),
                    deps=(def_dmas + rec_ops) if (k == 0 and g == 0) else [],
                    dma=True, tag=f"rsc{k}_{g}"))


        scat0_brg = gp_bridge(scat0, "scat0b")
        mk = {}
        scat_ops = []
        zd_all = p2st["zd_ops"]
        fr = {}

        def slot_front(s_):
            par = s_ % 2
            qr = s_ % 4
            met = sc.add(SY, lambda e, s_=s_, qr=qr: e.dma_start(
                out=SMET[:, qr * 4:(qr + 1) * 4],
                in_=slot_d.ap()[s_ * 128:(s_ + 1) * 128, :]),
                deps=[scat0_brg, mk.get(("stt", qr))] +
                     ([mk.get(("scat", 1 - par))] if SERIAL else []),
                dma=True, tag=f"met{s_}")
            cnv = sc.add(V, lambda e, qr=qr: e.tensor_copy(
                out=SIDU[:, qr * 2:qr * 2 + 1],
                in_=SMET[:, qr * 4:qr * 4 + 1]), deps=[met,
                mk.get(("zg", qr)), mk.get(("scat", qr))],
                tag=f"cnv{s_}a", risky=True)
            cnv2 = sc.add(V, lambda e, qr=qr: e.tensor_copy(
                out=SIDU[:, qr * 2 + 1:qr * 2 + 2],
                in_=SMET[:, qr * 4 + 2:qr * 4 + 3]), tag=f"cnv{s_}b",
                risky=True)
            zsl = 2048 + par * 1024
            zts = 6144 + par * 1024
            zg_ = sc.add(GP, lambda e, zsl=zsl, qr=qr: e.indirect_dma_start(
                out=MIX[:, zsl:zsl + 1024], out_offset=None,
                in_=zd_d.ap(), in_offset=IOA(
                    ap=SIDU[:, qr * 2:qr * 2 + 1], axis=0)),
                deps=[cnv, cnv2] + zd_all + [mk.get(("tr", par)),
                      mk.get(("wg2p", 0))],
                dma=True, tag=f"zg{s_}")
            mk[("zg", qr)] = zg_

            def tr_emit(e, zsl=zsl):
                for c in range(8):
                    last = e.transpose(PS_B[:, c * 128:(c + 1) * 128],
                                       MIX[:, zsl + c * 128:zsl + (c + 1) * 128],
                                       IDN[:])
                return last
            trs = sc.add(T, tr_emit,
                         deps=ps_w(["B6", "B7"], [zg_, idnb_cp]), tag=f"tr{s_}")
            ps_w_done(["B6", "B7"], trs)
            mk[("tr", par)] = trs
            evz = sc.add(A, lambda e, zts=zts: e.copy(
                S1[:, zts:zts + 1024], PS_B[:, 0:1024]),
                deps=ps_r(["B6", "B7"], [trs]) + [mk.get(("hid2", par))] +
                     ([aost["a_aev"]] if s_ < 2 else []),
                tag=f"evz{s_}")
            ps_r_done(["B6", "B7"], evz)
            fr[s_] = (zg_, trs, evz)

        def slot_back(s_):
            par = s_ % 2
            qr = s_ % 4
            zts = 6144 + par * 1024
            zg_, trs, evz = fr.pop(s_)
            gated = WG_MODE in (1, 3)
            wg1 = sc.add(GP, lambda e, s_=s_, gated=gated: e.indirect_dma_start(
                out=W1S[:], out_offset=None, in_=wd1_d.ap(),
                in_offset=IOA(ap=WIDXU[:, (s_ if gated else 64 + s_):
                                  (s_ if gated else 64 + s_) + 1], axis=0),
                bounds_check=(get_breg(e) if WG_MODE >= 1 else None),
                oob_is_err=(WG_MODE == 0)),
                deps=[w_iu, mk.get(("hid2", 0)), mk.get(("hid2", 1))] +
                     (pp_stores if s_ == 0 else []) +
                     ([zg_] if (WG_MODE == 3 and WG_QUIESCE) else []),
                dma=True, tag=f"wg1_{s_}")
            wg2 = sc.add(GP, lambda e, s_=s_, gated=gated: e.indirect_dma_start(
                out=W2S[:], out_offset=None, in_=wd2_d.ap(),
                in_offset=IOA(ap=WIDXU[:, (s_ if gated else 64 + s_):
                                  (s_ if gated else 64 + s_) + 1], axis=0),
                bounds_check=(get_breg(e) if WG_MODE >= 1 else None),
                oob_is_err=(WG_MODE == 0)),
                deps=[mk.get(("outs", 0)), mk.get(("outs", 1))] +
                     ([aost["a_aev"]] if s_ == 0 else []) +
                     ([zg_] if (WG_MODE == 3 and WG_QUIESCE) else []),
                dma=True, tag=f"wg2_{s_}")
            mk[("wg2p", 0)] = wg2
            b1p = None
            for e_ in range(E):
                if e_ == 0:
                    b1p = sc.add(V, lambda e, par=par, s_=s_: e.tensor_scalar(
                        out=B1SL[:, par * 16:(par + 1) * 16], in0=B1T[:, 0:16],
                        scalar1=MASKE2[:, s_:s_ + 1], scalar2=None,
                        op0=ALU.mult),
                        deps=[me_cp, b1_last, mk.get(("gelu", par))],
                        tag=f"b1s{s_}_0", risky=True)
                else:
                    b1p = sc.add(V, lambda e, e_=e_, par=par, s_=s_:
                        e.scalar_tensor_tensor(
                            out=B1SL[:, par * 16:(par + 1) * 16],
                            in0=B1T[:, e_ * 16:(e_ + 1) * 16],
                            scalar=MASKE2[:, e_ * NSLOT + s_:e_ * NSLOT + s_ + 1],
                            in1=B1SL[:, par * 16:(par + 1) * 16],
                            op0=ALU.mult, op1=ALU.add), tag=f"b1s{s_}_{e_}",
                        risky=True)

            def hid_emit(e, zts=zts, hclo=0, hchi=8):
                for hc in range(hclo, hchi):
                    for kc in range(8):
                        last = e.matmul(
                            PS_H[:, hc * 128:(hc + 1) * 128],
                            W1S[:, kc * 2048 + hc * 128: kc * 2048 + (hc + 1) * 128],
                            S1[:, zts + kc * 128: zts + (kc + 1) * 128],
                            start=(kc == 0), stop=(kc == 7))
                return last
            hid1 = sc.add(T, lambda e, zts=zts: hid_emit(e, zts, 0, 8),
                          deps=ps_w(["H01"], [wg1, evz]), tag=f"hid1_{s_}")
            ps_w_done(["H01"], hid1)
            hid2 = sc.add(T, lambda e, zts=zts: hid_emit(e, zts, 8, 16),
                          deps=ps_w(["H23"], []), tag=f"hid2_{s_}")
            ps_w_done(["H23"], hid2)
            mk[("hid2", par)] = hid2
            gl = None
            gelu_ops = []
            for hc in range(16):
                gl = sc.add(A, lambda e, hc=hc, par=par: e.activation(
                    out=S1[:, par * 2048 + hc * 128: par * 2048 + (hc + 1) * 128],
                    in_=PS_H[:, hc * 128:(hc + 1) * 128], func=AF.Gelu,
                    bias=B1SL[:, par * 16 + hc:par * 16 + hc + 1], scale=1.0),
                    deps=(ps_r(["H01"], [hid1, b1p, mk.get(("outs", par))] +
                               ([aost["a_aev"]] if s_ < 2 else []))
                          if hc == 0 else
                          (ps_r(["H23"], [hid2]) if hc == 8 else [])),
                    tag=f"gel{s_}_{hc}")
                gelu_ops.append(gl)
            ps_r_done(["H01"], gelu_ops[7])
            ps_r_done(["H23"], gelu_ops[15])
            mk[("gelu", par)] = gl

            mk[("bk", s_)] = (gl, wg2)

        def slot_back2(s_):
            par = s_ % 2
            qr = s_ % 4
            gl, wg2 = mk.pop(("bk", s_))

            def outs_emit(e, par=par):
                for n in range(2):
                    for hc in range(16):
                        last = e.matmul(
                            PS_A[:, n * 512:(n + 1) * 512],
                            S1[:, par * 2048 + hc * 128: par * 2048 + (hc + 1) * 128],
                            W2S[:, hc * 1024 + n * 512: hc * 1024 + (n + 1) * 512],
                            start=(hc == 0), stop=(hc == 15))
                return last
            outs = sc.add(T, outs_emit,
                          deps=ps_w(["A4", "A5"], [gl, wg2]), tag=f"outs{s_}")
            ps_w_done(["A4", "A5"], outs)
            mk[("outs", par)] = outs
            stt = sc.add(A, lambda e, par=par, qr=qr: e.activation(
                out=MIX[:, par * 1024:(par + 1) * 1024], in_=PS_A[:, 0:1024],
                func=AF.Identity, scale=SMET[:, qr * 4 + 1:qr * 4 + 2]),
                deps=ps_r(["A4", "A5"], [outs]) + [mk.get(("scat", par))],
                tag=f"stt{s_}")
            ps_r_done(["A4", "A5"], stt)
            mk[("stt", qr)] = stt
            sca = sc.add(SY, lambda e, par=par, s_=s_: e.dma_start(
                out=abd_d.ap()[s_ * 128:(s_ + 1) * 128, :],
                in_=MIX[:, par * 1024:(par + 1) * 1024]),
                deps=[stt], dma=True, tag=f"sca{s_}")
            mk[("scat", par)] = sca
            mk[("scat", qr)] = sca
            scat_ops.append(sca)


        slot_front(0)
        for s_ in range(NSLOT):
            slot_back(s_)
            if s_ + 1 < NSLOT:
                slot_front(s_ + 1)
            slot_back2(s_)

        scat_brg = gp_bridge(scat_ops, "scatb")
        if dbg:
            sc.add(GP, lambda e: e.dma_start(out=dbg_abd.ap(), in_=abd_d.ap()),
                   deps=[scat_brg], dma=True, tag="dbg_abd")
            sc.add(GP, lambda e: e.dma_start(out=dbg_slotd.ap(), in_=slot_d.ap()),
                   deps=[scat_brg], dma=True, tag="dbg_slotd")
        epi_ld = {}
        ch_eout = {}
        ch_es4 = {}

        def epi_stage_ops(g):
            par = g % 2
            aoff = 2048 * (1 - par)
            XNl = XN if par == 0 else SQX
            TMPl = TMP if par == 0 else G3B
            Gl = ZF if par == 0 else B3B
            st = {}

            def s_dma():
                st["xl"] = sc.add(SY, lambda e, g=g, par=par: e.dma_start(
                    out=XT[par][:], in_=x_d.ap()[g * 128:(g + 1) * 128, :]),
                    deps=[last_xt_user[par]], dma=True, tag=f"xepi{g}")
                st["la"] = sc.add(GP, lambda e, g=g, aoff=aoff:
                    e.indirect_dma_start(
                        out=MIX[:, aoff:aoff + 1024], out_offset=None,
                        in_=abd_d.ap(),
                        in_offset=IOA(ap=IRECU[:, g:g + 1], axis=0)),
                    deps=[scat_brg, epi_ld.get(par)], dma=True, tag=f"la{g}")
                st["lb"] = sc.add(GP, lambda e, g=g, aoff=aoff:
                    e.indirect_dma_start(
                        out=MIX[:, aoff + 1024:aoff + 2048], out_offset=None,
                        in_=abd_d.ap(),
                        in_offset=IOA(ap=IRECU[:, 16 + g:16 + g + 1], axis=0)),
                    deps=[], dma=True, tag=f"lb{g}")

            def s_ebmm():
                rot = g % 4
                epair = (["B6", "B7"], ["A4", "A5"], ["H01"], ["H23"])[rot]

                def eslice(n0, n1):
                    if rot == 0:
                        return PS_B[:, n0:n1]
                    if rot == 1:
                        return PS_A[:, n0:n1]
                    if rot == 2:
                        return PS_H[:, n0:n1]
                    return PS_H[:, 1024 + n0:1024 + n1]

                def ebmm_emit(e, g=g, eslice=eslice):
                    for n in range(2):
                        last = e.matmul(
                            eslice(n * 512, (n + 1) * 512),
                            WTRALL[0:8, g * 128:(g + 1) * 128],
                            B2S[:, n * 512:(n + 1) * 512], start=True, stop=True)
                    return last
                st["ebmm"] = sc.add(T, ebmm_emit,
                                    deps=ps_w(epair, [wtr_cp[g], ld_b2s]),
                                    tag=f"ebmm{g}")
                ps_w_done(epair, st["ebmm"])
                st["epair"] = epair
                st["epap"] = eslice(0, 1024)

            def s_exn():
                st["exn"] = sc.add(A, lambda e, g=g, par=par, XNl=XNl:
                    e.activation(
                        out=XNl[:], in_=XT[par][:], func=AF.Identity,
                        scale=RSTD1[:, g:g + 1], bias=NMR[:, g:g + 1]),
                    deps=[st["xl"], b_nmr1, ch_eout.get(par)], tag=f"exn{g}")

            def s_ef0():
                st["ef0"] = sc.add(V, lambda e, aoff=aoff, TMPl=TMPl:
                    e.tensor_tensor(
                        out=TMPl[:], in0=MIX[:, aoff:aoff + 1024],
                        in1=MIX[:, aoff + 1024:aoff + 2048], op=ALU.add),
                    deps=[st["la"], st["lb"]], tag=f"ef0_{g}")
                epi_ld[par] = st["ef0"]

            def s_ef1():
                st["ef1"] = sc.add(V, lambda e, TMPl=TMPl: e.tensor_tensor(
                    out=TMPl[:], in0=TMPl[:], in1=st["epap"], op=ALU.add),
                    deps=ps_r(st["epair"], [st["ebmm"]]), tag=f"ef1_{g}")
                ps_r_done(st["epair"], st["ef1"])

            def s_ef2():
                st["ef2"] = sc.add(V, lambda e, TMPl=TMPl, XNl=XNl:
                    e.tensor_tensor(
                        out=TMPl[:], in0=TMPl[:], in1=XNl[:], op=ALU.add),
                    deps=[st["exn"]], tag=f"ef2_{g}")

            def s_eg():
                st["eg"] = sc.add(A, lambda e, TMPl=TMPl, Gl=Gl: e.activation(
                    out=Gl[:], in_=TMPl[:], func=AF.Silu),
                    deps=[st["ef2"], ch_es4.get(par)], tag=f"eg{g}")

            def s_ea():
                if g == 15:
                    st["ea"] = sc.add(V, lambda e, XNl=XNl: e.tensor_tensor(
                        out=XNl[:], in0=XNl[:], in1=APAD[:], op=ALU.add),
                        deps=[aost["apad_dma"], st["exn"]], tag=f"ea{g}")

            def s_es1():
                st["es1"] = sc.add(A, lambda e, g=g, XNl=XNl, TMPl=TMPl:
                    e.activation(
                        out=TMPl[:], in_=XNl[:], func=AF.Identity,
                        scale=RSTD2[:, g:g + 1], bias=NMR[:, NT + g:NT + g + 1]),
                    deps=[b_nmr2, st["eg"]] +
                         ([st["ea"], aost["nmr2f"]] if g == 15 else []),
                    tag=f"es1_{g}")

            def s_tailv():
                sc.add(V, lambda e, XNl=XNl, TMPl=TMPl: e.tensor_tensor(
                    out=XNl[:], in0=TMPl[:], in1=G2B[:], op=ALU.mult),
                    deps=[bcast_cp[2], st["es1"]], tag=f"es2_{g}")
                sc.add(V, lambda e, XNl=XNl, TMPl=TMPl: e.tensor_tensor(
                    out=TMPl[:], in0=XNl[:], in1=B2LB[:], op=ALU.add),
                    deps=[bcast_cp[3]], tag=f"es3_{g}")
                es4 = sc.add(V, lambda e, XNl=XNl, TMPl=TMPl, Gl=Gl:
                    e.tensor_tensor(
                        out=XNl[:], in0=TMPl[:], in1=Gl[:], op=ALU.mult),
                    deps=[st["eg"]], tag=f"es4_{g}")
                ch_es4[par] = es4
                epo = sc.add(V, lambda e, par=par, XNl=XNl: e.tensor_tensor(
                    out=OUTB[:, par * DM:(par + 1) * DM], in0=XNl[:],
                    in1=XT[par][:], op=ALU.add),
                    deps=[st["xl"]] + ([st_ops[g - 2]] if g >= 2 else []),
                    tag=f"eout{g}")
                ch_eout[par] = epo
                last_xt_user[par] = epo
                st_ops[g] = sc.add(SY, lambda e, g=g, par=par: e.dma_start(
                    out=y_d.ap()[g * 128:(g + 1) * 128, :],
                    in_=OUTB[:, par * DM:(par + 1) * DM]),
                    deps=[epo], dma=True, tag=f"st{g}")

            return [s_dma, s_ebmm, s_exn, s_ef0, s_ef1, s_ef2, s_eg, s_ea,
                    s_es1, s_tailv]

        for g0 in range(0, NT, 2):
            stages0 = epi_stage_ops(g0)
            stages1 = epi_stage_ops(g0 + 1)
            for f0, f1 in zip(stages0, stages1):
                f0()
                f1()

        sc.add(SY, lambda e: None,
               deps=[st_ops[g] for g in range(NT) if st_ops[g] is not None],
               tag="fin")

        sc.emit(block, sems)

    return nc


_NC_CACHE = None


def _get_nc():
    global _NC_CACHE
    if _NC_CACHE is None:
        _NC_CACHE = build_nc()
    return _NC_CACHE


def kernel(L0_attn_x, L0_attn_W_Q, L0_attn_W_K, L0_attn_W_V, L0_attn_W_Kp,
           L0_attn_W_O, L0_attn_ln_gamma, L0_attn_ln_beta, L0_ffn_ln_g,
           L0_ffn_ln_b, L0_ffn_rW, L0_ffn_rb, L0_ffn_W1, L0_ffn_b1,
           L0_ffn_W2, L0_ffn_b2):
    def f32(a):
        return np.ascontiguousarray(np.asarray(a, dtype=np.float32))

    x = f32(L0_attn_x)
    shared = {
        "wq": f32(L0_attn_W_Q), "wk": f32(L0_attn_W_K), "wkp": f32(L0_attn_W_Kp),
        "wv": f32(L0_attn_W_V), "wo": f32(L0_attn_W_O),
        "g2": f32(L0_attn_ln_gamma), "be2": f32(L0_attn_ln_beta),
        "g3": f32(L0_ffn_ln_g), "b3": f32(L0_ffn_ln_b),
        "rw": f32(L0_ffn_rW), "rb": f32(L0_ffn_rb),
        "w1": f32(L0_ffn_W1), "b1": f32(L0_ffn_b1),
        "w2": f32(L0_ffn_W2), "b2": f32(L0_ffn_b2),
    }
    t = np.arange(WIN)[:, None, None]
    j = np.arange(WIN)[None, :, None]
    k = np.arange(WIN)[None, None, :]
    mask = np.where((j <= t) & (k <= t), 0.0, NEG_BIG).astype(np.float32)
    shared["mask"] = np.ascontiguousarray(mask.reshape(WIN, WIN * WIN))
    shared["ident"] = np.eye(128, dtype=np.float32)
    shared["ltri"] = np.triu(np.ones((128, 128), np.float32), 1)
    shared["tidt"] = np.ascontiguousarray(
        (np.arange(NT)[None, :] * 128 + np.arange(128)[:, None]).astype(np.float32))
    emaj = np.zeros((128, 128), np.float32)
    for e in range(E):
        emaj[:, e * NT:(e + 1) * NT] = e
    shared["emaj"] = emaj
    d4 = np.zeros((128, 4), np.float32)
    d4[:, 2] = 4096 + np.arange(128)
    shared["def4"] = d4
    sio = np.full((1, 64), 1e9, np.float32)
    sio[0, :NSLOT] = np.arange(NSLOT) * 128
    shared["sio"] = sio

    nc = _get_nc()
    in_maps = []
    for b in range(8):
        m = dict(shared)
        m["x"] = np.ascontiguousarray(x[b])
        in_maps.append(m)
    res = run_bass_kernel_spmd(nc, in_maps, core_ids=list(range(8)))
    return np.stack([r["y"] for r in res.results], axis=0)












